# version 16
# baseline (speedup 1.0000x reference)
import numpy as np
import concourse.bass as bass
import concourse.mybir as mybir
from concourse.bass_utils import run_bass_kernel_spmd
from contextlib import ExitStack

F32 = mybir.dt.float32
BF16 = mybir.dt.bfloat16
I8 = mybir.dt.int8
I32 = mybir.dt.int32
AF = mybir.ActivationFunctionType
ALU = mybir.AluOpType
AXX = mybir.AxisListType.X

DT16 = BF16
T = 2048
C = 1024
H = 16
D = 64
BL = 2
NTOK = BL * T
NT = NTOK // 128
NTB = T // 128
QT = 16


class Sem:
    def __init__(self, nc, ctx, name):
        self.h = ctx.enter_context(nc.semaphore(name))
        self.n = 0

    def inc(self, instr, k=1):
        instr.then_inc(self.h, k)
        self.n += k
        return self.n


def build():
    nc = bass.Bass("TRN2", target_bir_lowering=False, detect_race_conditions=False)

    x0 = nc.dram_tensor("x0", [NTOK, C], F32, kind="ExternalInput").ap()
    wattn_m = nc.dram_tensor("wattn_m", [24, 128, C], F32, kind="ExternalInput").ap()
    battn = nc.dram_tensor("battn", [3 * C, 1], F32, kind="ExternalInput").ap()
    wproj_m = nc.dram_tensor("wproj_m", [8, 128, C], F32, kind="ExternalInput").ap()
    bprow = nc.dram_tensor("bprow", [1, C], F32, kind="ExternalInput").ap()
    grow = nc.dram_tensor("grow", [1, 1], F32, kind="ExternalInput").ap()
    w8t = nc.dram_tensor("w8t", [8, 128, C], I8, kind="ExternalInput").ap()
    wsrow = nc.dram_tensor("wsrow", [1, C], F32, kind="ExternalInput").ap()
    fbrow = nc.dram_tensor("fbrow", [1, C], F32, kind="ExternalInput").ap()
    out = nc.dram_tensor("out", [NTOK, C], F32, kind="ExternalOutput").ap()
    xp = nc.dram_tensor("xp", [NTOK, C], F32, kind="Internal").ap()
    cc_in = nc.dram_tensor("cc_in", [1, 8], F32, kind="Internal").ap()
    cc_out = nc.dram_tensor("cc_out", [1, 8], F32, kind="Internal").ap()

    ctx = ExitStack()
    with ctx:
        def sb(name, shape, dt=F32):
            return ctx.enter_context(nc.sbuf_tensor(name, shape, dt))

        qT = sb("qT", [128, 8 * T], DT16)
        kT = sb("kT", [128, 8 * T], DT16)
        VT = sb("VT", [128, NTB * C], DT16)
        ySB = sb("ySB", [128, 8 * T], DT16)
        wsb = sb("wsb", [128, 8 * C], DT16)
        ident = sb("ident", [128, 128], DT16)
        identf = sb("identf", [128, 128], F32)
        maskq = sb("maskq", [128, 128], DT16)
        battn16 = sb("battn16", [128, 24])
        sgb8 = sb("sgb8", [128, 8])
        onescol = sb("onescol", [1, 128])
        grw = sb("grw", [1, 8])
        sgcol = sb("sgcol", [128, 1])
        iccol = sb("iccol", [128, 1])
        jecol = sb("jecol", [128, 1])
        isxcol = sb("isxcol", [128, 1])
        bprw = sb("bprw", [1, C])
        wsrw = sb("wsrw", [1, C])
        fbrw = sb("fbrw", [1, C])
        ccb = sb("ccb", [1, 8])
        BPROJ = sb("BPROJ", [128, C])
        FBX = sb("FBX", [128, C])
        FSC = sb("FSC", [128, C])
        mb2 = sb("mb2", [128, NT])
        rs2 = sb("rs2", [128, NT])
        lmx = sb("lmx", [128, NT])
        gmxc = sb("gmxc", [128, 1])
        mado = sb("mado", [128, QT])
        den16 = sb("den16", [128, QT])
        inv16 = sb("inv16", [128, QT])
        stat = sb("stat", [128, 8])
        diag = sb("diag", [128, 2 * 128], BF16)
        w8stg = sb("w8stg", [128, C], I8)

        s_sd = Sem(nc, ctx, "s_sd")
        s_gd = Sem(nc, ctx, "s_gd")
        s_pe = Sem(nc, ctx, "s_pe")
        s_ac = Sem(nc, ctx, "s_ac")
        s_ve = Sem(nc, ctx, "s_ve")
        s_gp = Sem(nc, ctx, "s_gp")
        s_st = Sem(nc, ctx, "s_st")
        s_cc = Sem(nc, ctx, "s_cc")
        s_ph = Sem(nc, ctx, "s_ph")
        s_te = Sem(nc, ctx, "s_te")
        M = {}
        scsc = sb("scsc", [128, 2])
        vjk = sb("vjk", [128, 2])

        blk = ctx.enter_context(nc.Block())

        pe_pend = [0]

        class PeProxy:
            def __init__(self, t):
                self.t = t

            def _att(self, ins):
                if pe_pend[0]:
                    ins.then_inc(s_pe.h, pe_pend[0])
                    s_pe.n += pe_pend[0]
                    pe_pend[0] = 0
                return ins

            def matmul(self, *a, **k):
                return self._att(self.t.matmul(*a, **k))

            def transpose(self, *a, **k):
                return self._att(self.t.transpose(*a, **k))

            def wait_ge(self, *a):
                return self.t.wait_ge(*a)

            def sem_inc(self, *a):
                return self.t.sem_inc(*a)

            def drain(self):
                return self.t.drain()

            def mark(self, key):
                pe_pend[0] += 1
                M[key] = s_pe.n + pe_pend[0]

            def mark_drain(self, key):
                ins = self.t.drain()
                k = pe_pend[0] + 1
                ins.then_inc(s_pe.h, k)
                s_pe.n += k
                pe_pend[0] = 0
                M[key] = s_pe.n

            def flush(self):
                if pe_pend[0]:
                    ins = self.t.drain()
                    ins.then_inc(s_pe.h, pe_pend[0])
                    s_pe.n += pe_pend[0]
                    pe_pend[0] = 0

        def TB(f):
            blk.tensor(lambda t: f(PeProxy(t)))

        def sc_mark(sc, key):
            ins = sc.copy(scsc[:, 0:1], scsc[:, 1:2])
            s_ac.inc(ins)
            M[key] = s_ac.n

        def vspace(v):
            v.tensor_copy(vjk[:, 0:1], vjk[:, 1:2])

        def barrier():
            tgt = s_ph.n + 5

            def _t(e):
                e.sem_inc(s_ph.h, 1)
                e.wait_ge(s_ph.h, tgt)

            def _tpe(t):
                PeProxy(t).flush()
                _t(t)
            blk.tensor(_tpe)
            blk.scalar(_t)
            blk.vector(_t)
            blk.gpsimd(_t)
            blk.sync(_t)
            s_ph.n = tgt

        def ph0_sync(sync):
            ins = sync.dma_start(out=battn16[:],
                                 in_=battn.rearrange("(a b) c -> b (a c)", b=128),
                                 allow_slow_non_contiguous=True)
            s_sd.inc(ins, 16)
            for ap_out, ap_in in [
                (grw[0:1, 0:1], grow), (bprw[:], bprow),
                (wsrw[:], wsrow), (fbrw[:], fbrow)]:
                ins = sync.dma_start(out=ap_out, in_=ap_in)
                s_sd.inc(ins, 16)
            M["ph0_sd"] = s_sd.n
        blk.sync(ph0_sync)

        def ph0_gp(gp):
            for tile, fillv, cmp_op, selfill in [
                    (ident, 0.0, ALU.not_equal, 1.0),
                    (identf, 0.0, ALU.not_equal, 1.0),
                    (maskq, 1.0, ALU.is_ge, 0.0)]:
                ins = gp.memset(tile[:], fillv)
                s_gp.inc(ins)
                gp.wait_ge(s_gp.h, s_gp.n)
                ins = gp.affine_select(out=tile[:], in_=tile[:], compare_op=cmp_op,
                                       fill=selfill, base=0, pattern=[[-1, 128]],
                                       channel_multiplier=1)
                s_gp.inc(ins)
                gp.wait_ge(s_gp.h, s_gp.n)
            gp.memset(vjk[:], 0.0)
            gp.memset(scsc[:], 0.0)
            ins = gp.memset(onescol[:], 1.0)
            s_gp.inc(ins)
            M["ph0_gp"] = s_gp.n
        blk.gpsimd(ph0_gp)

        def ph0_vec(v):
            v.wait_ge(s_sd.h, M["ph0_sd"])
            v.wait_ge(s_gp.h, M["ph0_gp"])
            v.tensor_scalar(grw[0:1, 1:2], grw[0:1, 0:1], -1.0, None, ALU.mult)
            vspace(v)
            vspace(v)
            v.tensor_tensor(grw[0:1, 2:3], grw[0:1, 0:1], grw[0:1, 1:2], ALU.max)
            vspace(v)
            vspace(v)
            v.tensor_scalar(grw[0:1, 3:4], grw[0:1, 2:3], 1e-30, None, ALU.max)
            vspace(v)
            vspace(v)
            v.tensor_scalar(grw[0:1, 4:5], grw[0:1, 3:4], 1024.0, None, ALU.mult)
            vspace(v)
            vspace(v)
            v.reciprocal(grw[0:1, 5:6], grw[0:1, 4:5])
            v.reciprocal(grw[0:1, 6:7], grw[0:1, 3:4])
            vspace(v)
            vspace(v)
            ins = v.tensor_scalar(grw[0:1, 7:8], grw[0:1, 6:7], 1e-6, None, ALU.mult)
            s_ve.inc(ins)
            M["ph0_g"] = s_ve.n
        blk.vector(ph0_vec)

        def ph0_sc(sc):
            sc.wait_ge(s_ve.h, M["ph0_g"])
            ins = sc.sign(grw[0:1, 0:1], grw[0:1, 0:1])
            s_ac.inc(ins)
            M["ph0_sg"] = s_ac.n
        blk.scalar(ph0_sc)

        with nc.psum_tensor("pbc", [128, 4], F32) as pbc, \
             nc.psum_tensor("pmat", [128, 2 * C], F32) as pmat:

            def ph0_pe(t):
                t.wait_ge(s_ac.h, M["ph0_sg"])
                t.matmul(pbc[:, 0:1], onescol[:], grw[0:1, 0:1], start=True, stop=True)
                t.matmul(pbc[:, 1:2], onescol[:], grw[0:1, 5:6], start=True, stop=True)
                t.matmul(pbc[:, 2:3], onescol[:], grw[0:1, 7:8], start=True, stop=True)
                for i in range(2):
                    t.matmul(pmat[:, i * 512:(i + 1) * 512], onescol[:],
                             bprw[0:1, i * 512:(i + 1) * 512], start=True, stop=True)
                    ins = t.matmul(pmat[:, C + i * 512:C + (i + 1) * 512], onescol[:],
                                   fbrw[0:1, i * 512:(i + 1) * 512], start=True, stop=True)
                t.mark_drain("ph0_mat")
            TB(ph0_pe)

            def ph0_vec2(v):
                v.wait_ge(s_pe.h, M["ph0_mat"])
                v.tensor_copy(sgcol[:], pbc[:, 0:1])
                v.tensor_copy(iccol[:], pbc[:, 1:2])
                v.tensor_copy(jecol[:], pbc[:, 2:3])
                v.tensor_copy(BPROJ[:], pmat[:, 0:C])
                v.tensor_copy(FBX[:], pmat[:, C:2 * C])
                ins = v.tensor_scalar(sgb8[:], battn16[:, 0:8], sgcol[:], 0.0,
                                      ALU.mult, ALU.add)
                s_ve.inc(ins)
                M["ph0_cols"] = s_ve.n
            blk.vector(ph0_vec2)

        def ph0_gp2(gp):
            for m in range(8):
                ins = gp.dma_start(out=wsb[:, m * C:(m + 1) * C], in_=wproj_m[m])
                s_gd.inc(ins, 16)
            M["wproj_ld"] = s_gd.n
        blk.gpsimd(ph0_gp2)

        barrier()

        ySB3 = ySB[:].rearrange("p (c n) -> p c n", n=T)

        for b in range(BL):
            tok0 = b * T

            with nc.sbuf_tensor(f"x0t{b}", [128, 2 * C], F32) as x0t, \
                 nc.sbuf_tensor(f"lnscr{b}", [128, C], F32) as lnscr, \
                 nc.sbuf_tensor(f"ln1t{b}", [128, 2 * C], DT16) as ln1t, \
                 nc.sbuf_tensor(f"lncols{b}", [128, 8], F32) as lncols, \
                 nc.psum_tensor(f"ptr1{b}", [128, 2 * C], DT16) as ptr1:

                for tt in range(NTB):
                    par = tt % 2
                    row = tok0 + tt * 128

                    def f_sync(sync):
                        if tt >= 2:
                            sync.wait_ge(s_ac.h, M[f"ln1n_{b}_{tt-2}"])
                        ins = sync.dma_start(out=x0t[:, par * C:(par + 1) * C],
                                             in_=x0[row:row + 128, :])
                        s_sd.inc(ins, 16)
                        M[f"x0_{b}_{tt}"] = s_sd.n
                    blk.sync(f_sync)

                    def f_sc1(sc):
                        sc.wait_ge(s_sd.h, M[f"x0_{b}_{tt}"])
                        sc.activation(lnscr[:], x0t[:, par * C:(par + 1) * C],
                                      AF.Square, accum_out=lncols[:, 1:2])
                        sc_mark(sc, f"ssq_{b}_{tt}")
                    blk.scalar(f_sc1)

                    def f_ve1(v):
                        v.wait_ge(s_sd.h, M[f"x0_{b}_{tt}"])
                        v.reduce_sum(lncols[:, 0:1], x0t[:, par * C:(par + 1) * C], AXX)
                        v.wait_ge(s_ac.h, M[f"ssq_{b}_{tt}"])
                        vspace(v)
                        v.tensor_scalar(lncols[:, 2:3], lncols[:, 0:1], 1.0 / C, None, ALU.mult)
                        vspace(v)
                        v.tensor_tensor(lncols[:, 3:4], lncols[:, 2:3], lncols[:, 2:3], ALU.mult)
                        vspace(v)
                        v.scalar_tensor_tensor(lncols[:, 4:5], lncols[:, 1:2], 1.0 / C,
                                               lncols[:, 3:4], ALU.mult, ALU.subtract)
                        vspace(v)
                        ins = v.tensor_scalar(lncols[:, 4:5], lncols[:, 4:5], 1e-5, None, ALU.add)
                        s_ve.inc(ins)
                        M[f"var_{b}_{tt}"] = s_ve.n
                    blk.vector(f_ve1)

                    def f_sc2(sc):
                        sc.wait_ge(s_ve.h, M[f"var_{b}_{tt}"])
                        ins = sc.activation(lncols[:, 5:6], lncols[:, 4:5], AF.Sqrt)
                        s_ac.inc(ins)
                        M[f"sqv_{b}_{tt}"] = s_ac.n
                    blk.scalar(f_sc2)

                    def f_ve2(v):
                        v.wait_ge(s_ac.h, M[f"sqv_{b}_{tt}"])
                        v.reciprocal(lncols[:, 6:7], lncols[:, 5:6])
                        vspace(v)
                        ins = v.tensor_scalar(lncols[:, 7:8], lncols[:, 2:3],
                                              lncols[:, 6:7], -1.0, ALU.mult, ALU.mult)
                        s_ve.inc(ins)
                        M[f"rstd_{b}_{tt}"] = s_ve.n
                    blk.vector(f_ve2)

                    def f_sc3(sc):
                        sc.wait_ge(s_ve.h, M[f"rstd_{b}_{tt}"])
                        if tt >= 2:
                            sc.wait_ge(s_pe.h, M[f"ln1tr_{b}_{tt-2}"])
                        sc.activation(ln1t[:, par * C:(par + 1) * C],
                                      x0t[:, par * C:(par + 1) * C], AF.Identity,
                                      bias=lncols[:, 7:8], scale=lncols[:, 6:7])
                        sc_mark(sc, f"ln1n_{b}_{tt}")
                    blk.scalar(f_sc3)

                    def f_pe(t):
                        t.wait_ge(s_ac.h, M[f"ln1n_{b}_{tt}"])
                        if tt >= 2:
                            t.wait_ge(s_ve.h, M[f"ln1e_{b}_{tt-2}"])
                        for c in range(8):
                            ins = t.transpose(ptr1[:, par * C + c * 128: par * C + (c + 1) * 128],
                                              ln1t[:, par * C + c * 128: par * C + (c + 1) * 128],
                                              ident[:])
                        t.mark_drain(f"ln1tr_{b}_{tt}")
                    TB(f_pe)

                    def f_ve3(v):
                        v.wait_ge(s_pe.h, M[f"ln1tr_{b}_{tt}"])
                        src = ptr1[:, par * C:(par + 1) * C].rearrange("p (c n) -> p c n", n=128)
                        dst = ySB3[:, :, tt * 128:(tt + 1) * 128]
                        ins = v.tensor_copy(dst, src)
                        s_ve.inc(ins)
                        M[f"ln1e_{b}_{tt}"] = s_ve.n
                    blk.vector(f_ve3)

                M[f"ln1done_{b}"] = s_ve.n

            with nc.sbuf_tensor(f"wcol{b}", [128, 2 * C], DT16) as wcol, \
                 nc.sbuf_tensor(f"kraw{b}", [128, T], F32) as kraw, \
                 nc.sbuf_tensor(f"vF{b}", [128, 2 * T], DT16) as vF, \
                 nc.psum_tensor(f"pqk{b}", [128, T], F32) as pqk, \
                 nc.psum_tensor(f"ptrv{b}", [128, 2 * C], DT16) as ptrv:

                def emit_vtrans(t, mc):
                    mslot = (16 + mc) % 2
                    if mc >= 1:
                        t.wait_ge(s_ve.h, M[f"vtre_{b}_{mc-1}"])
                    for tb in range(NTB):
                        ins = t.transpose(
                            ptrv[:, (tb // 8) * C + (tb % 8) * 128:(tb // 8) * C + (tb % 8) * 128 + 128],
                            vF[:, mslot * T + tb * 128: mslot * T + tb * 128 + 128],
                            ident[:])
                        if tb == 7:
                            t.mark(f"vtr_{b}_{mc}_0")
                        elif tb == 15:
                            t.mark_drain(f"vtr_{b}_{mc}_1")

                def emit_vtrans_evac(v, mc):
                    for half in range(2):
                        v.wait_ge(s_pe.h, M[f"vtr_{b}_{mc}_{half}"])
                        for tb in range(half * 8, half * 8 + 8):
                            ins = v.tensor_copy(
                                VT[:, tb * C + mc * 128: tb * C + mc * 128 + 128],
                                ptrv[:, (tb // 8) * C + (tb % 8) * 128:(tb // 8) * C + (tb % 8) * 128 + 128])
                        s_ve.inc(ins)
                    M[f"vtre_{b}_{mc}"] = s_ve.n

                for m in range(24):
                    wpar = m % 2

                    def f_gp(gp):
                        if m >= 2:
                            gp.wait_ge(s_pe.h, M[f"qkvmm_{b}_{m-2}"])
                        else:
                            gp.wait_ge(s_ve.h, M[f"ln1done_{b}"])
                        ins = gp.dma_start(out=wcol[:, wpar * C:(wpar + 1) * C], in_=wattn_m[m])
                        s_gd.inc(ins, 16)
                        M[f"wc_{b}_{m}"] = s_gd.n
                    blk.gpsimd(f_gp)

                    def f_pe(t):
                        t.wait_ge(s_gd.h, M[f"wc_{b}_{m}"])
                        if m == 0:
                            t.wait_ge(s_ve.h, M[f"ln1done_{b}"])
                        if m >= 1:
                            t.wait_ge(s_ac.h, M[f"qkve_{b}_{m-1}"])
                        for k in range(8):
                            for hf in range(4):
                                ins = t.matmul(pqk[:, hf * 512:(hf + 1) * 512],
                                               wcol[:, wpar * C + k * 128: wpar * C + (k + 1) * 128],
                                               ySB3[:, k, hf * 512:(hf + 1) * 512],
                                               start=(k == 0), stop=(k == 7))
                        t.mark_drain(f"qkvmm_{b}_{m}")
                        if m - 1 >= 16:
                            emit_vtrans(t, m - 1 - 16)
                    TB(f_pe)

                    def f_sc(sc):
                        sc.wait_ge(s_pe.h, M[f"qkvmm_{b}_{m}"])
                        if m < 8:
                            ins = sc.activation(qT[:, m * T:(m + 1) * T], pqk[:], AF.Identity,
                                                bias=sgb8[:, m:m + 1], scale=sgcol[:])
                        elif m < 16:
                            if m >= 9:
                                sc.wait_ge(s_ve.h, M[f"kTd_{b}_{m-1}"])
                            ins = sc.activation(kraw[:], pqk[:], AF.Identity,
                                                bias=battn16[:, m:m + 1], scale=1.0)
                        else:
                            if m >= 18:
                                sc.wait_ge(s_pe.h, M[f"vtr_{b}_{m-2-16}_{1}"])
                            ins = sc.activation(vF[:, (m % 2) * T:(m % 2) * T + T], pqk[:],
                                                AF.Identity, bias=battn16[:, m:m + 1], scale=1.0)
                        sc_mark(sc, f"qkve_{b}_{m}")
                    blk.scalar(f_sc)

                    if 8 <= m < 16:
                        def f_ve(v):
                            v.wait_ge(s_ac.h, M[f"qkve_{b}_{m}"])
                            v.reduce_sum(stat[:, 0:1], kraw[:], AXX)
                            vspace(v)
                            vspace(v)
                            v.tensor_scalar(stat[:, 2:3], stat[:, 0:1], 1.0 / T, None, ALU.mult)
                            vspace(v)
                            vspace(v)
                            v.tensor_scalar(kT[:, (m - 8) * T:(m - 7) * T], kraw[:],
                                            stat[:, 2:3], None, ALU.subtract)
                            ins = v.tensor_copy(vjk[:, 0:1], vjk[:, 1:2])
                            s_ve.inc(ins)
                            M[f"kTd_{b}_{m}"] = s_ve.n
                        blk.vector(f_ve)

                    if m - 1 >= 16:
                        def f_vev(v):
                            v.wait_ge(s_ac.h, M[f"qkve_{b}_{m}"]) if False else None
                            emit_vtrans_evac(v, m - 1 - 16)
                        blk.vector(f_vev)

                def f_pe_last(t):
                    t.wait_ge(s_ac.h, M[f"qkve_{b}_{23}"])
                    emit_vtrans(t, 7)
                TB(f_pe_last)

                def f_ve_last(v):
                    emit_vtrans_evac(v, 7)
                blk.vector(f_ve_last)

                barrier()

            with nc.sbuf_tensor(f"v1{b}", [128, 2 * T], DT16) as v1, \
                 nc.sbuf_tensor(f"v2{b}", [128, T], DT16) as v2, \
                 nc.sbuf_tensor(f"r4{b}", [128, 2 * T], BF16) as r4, \
                 nc.sbuf_tensor(f"aTS{b}", [128, 2 * T], DT16) as aTS, \
                 nc.psum_tensor(f"U{b}", [128, T], F32) as U, \
                 nc.psum_tensor(f"Tp{b}", [128, 1024], F32) as Tp, \
                 nc.psum_tensor(f"Yp{b}", [128, 1024], F32) as Yp:

                for h in range(H):
                    pair, parity = h // 2, h % 2
                    qoff = pair * T
                    tp_col = (0, 64) if parity else None
                    tp_row = (64, 0) if parity else None

                    base_ac = s_ac.n
                    for qt in range(QT):
                        M[f"p1_{b}_{h}_{qt}"] = base_ac + qt + 1
                    nve = s_ve.n
                    nte = s_te.n
                    te_bases = {}
                    for i in range(QT + 3):
                        qt, qa, qv = i, i - 2, i - 3
                        if 0 <= qa <= 15:
                            te_bases[qa] = nte
                            nte += (qa + 4) // 4
                            M[f"aTSe_{b}_{h}_{qa}"] = nte
                        if 0 <= qv <= 15 and (qv % 4 == 3 or qv == 15):
                            nve += 1
                            M[f"yev_{b}_{h}_{qv//4}"] = nve
                        if qt <= 15:
                            nve += 1
                            M[f"vq_{b}_{h}_{qt}"] = nve
                    pred_ve, pred_te, pred_ac = nve, nte, base_ac + QT

                    def f_pe(t):
                        for i in range(QT + 3):
                            qt, qa, qv = i, i - 2, i - 3
                            if i >= 16:
                                t.flush()
                            if qt <= 15:
                                if qt >= 1:
                                    t.wait_ge(s_ac.h, M[f"p1_{b}_{h}_{qt-1}"])
                                elif h >= 1:
                                    t.wait_ge(s_ac.h, M[f"p1_{b}_{h-1}_15"])
                                for cc in range(4):
                                    t.matmul(U[:, cc * 512:(cc + 1) * 512],
                                             qT[parity * 64:parity * 64 + 64,
                                                qoff + qt * 128: qoff + qt * 128 + 128],
                                             kT[parity * 64:parity * 64 + 64,
                                                qoff + cc * 512: qoff + (cc + 1) * 512],
                                             start=True, stop=True, tile_position=tp_row)
                                if qt <= 1:
                                    t.mark_drain(f"U_{b}_{h}_{qt}")
                                else:
                                    t.mark(f"U_{b}_{h}_{qt}")
                            if 0 <= qa <= 15:
                                te_base = te_bases[qa]
                                t.wait_ge(s_ve.h, M[f"vq_{b}_{h}_{qa}"])
                                if te_base > 0:
                                    t.wait_ge(s_te.h, te_base)
                                for kb in range(qa + 1):
                                    j = kb // 4
                                    if j >= 2:
                                        t.wait_ge(s_te.h, te_base + j - 1)
                                    t.matmul(
                                        Tp[:, (j % 2) * 512 + (kb % 4) * 128:
                                           (j % 2) * 512 + (kb % 4) * 128 + 128],
                                        r4[:, (qa % 2) * T + kb * 128:(qa % 2) * T + kb * 128 + 128],
                                        diag[:, (qa % 2) * 128:(qa % 2) * 128 + 128],
                                        start=True, stop=True)
                                    if kb % 4 == 3 or kb == qa:
                                        t.mark(f"attT_{b}_{h}_{qa}_{kb//4}")
                            if 0 <= qv <= 15:
                                g = qv // 4
                                t.wait_ge(s_te.h, M[f"aTSe_{b}_{h}_{qv}"])
                                if qv % 4 == 0:
                                    if g >= 2:
                                        t.wait_ge(s_ve.h, M[f"yev_{b}_{h}_{g-2}"])
                                    elif h >= 1:
                                        t.wait_ge(s_ve.h, M[f"yev_{b}_{h-1}_{g+2}"])
                                for kb in range(qv + 1):
                                    first = (qv % 4 == 0 and kb == 0)
                                    last = (kb == qv and (qv % 4 == 3 or qv == 15))
                                    t.matmul(
                                        Yp[parity * 64:parity * 64 + 64,
                                           (g % 2) * 512 + (qv % 4) * 128:
                                           (g % 2) * 512 + (qv % 4) * 128 + 128],
                                        VT[:, kb * C + h * D: kb * C + h * D + D],
                                        aTS[:, (qv % 2) * T + kb * 128:(qv % 2) * T + (kb + 1) * 128],
                                        start=first, stop=last, tile_position=tp_col)
                                t.mark(f"attV_{b}_{h}_{qv}")
                    TB(f_pe)

                    def f_sc(sc):
                        for qt in range(QT):
                            sc.wait_ge(s_pe.h, M[f"U_{b}_{h}_{qt}"])
                            if qt >= 2:
                                sc.wait_ge(s_ve.h, M[f"vq_{b}_{h}_{qt-2}"])
                            elif h >= 1:
                                sc.wait_ge(s_ve.h, M[f"vq_{b}_{h-1}_{14+qt}"])
                            sc.activation(v1[:, (qt % 2) * T:(qt % 2) * T + T], U[:],
                                          AF.Relu, accum_out=mado[:, qt:qt + 1])
                            sc_mark(sc, f"p1_{b}_{h}_{qt}")
                    n0 = s_ac.n
                    s_ac.n = base_ac
                    blk.scalar(f_sc)
                    assert s_ac.n == pred_ac, (s_ac.n, pred_ac)

                    def f_ve(v):
                        for i in range(QT + 3):
                            qt, qa, qv = i, i - 2, i - 3
                            if 0 <= qa <= 15:
                                if qa >= 2:
                                    v.wait_ge(s_pe.h, M[f"attV_{b}_{h}_{qa-2}"])
                                elif h >= 1:
                                    v.wait_ge(s_pe.h, M[f"attV_{b}_{h-1}_{14+qa}"])
                                nsg = (qa + 4) // 4
                                for j in range(nsg):
                                    v.wait_ge(s_pe.h, M[f"attT_{b}_{h}_{qa}_{j}"])
                                    lo = j * 4 * 128
                                    hi = min((j + 1) * 4, qa + 1) * 128
                                    ins = v.tensor_copy(aTS[:, (qa % 2) * T + lo:(qa % 2) * T + hi],
                                                        Tp[:, (j % 2) * 512:(j % 2) * 512 + hi - lo])
                                    s_te.inc(ins)
                                assert s_te.n == M[f"aTSe_{b}_{h}_{qa}"]
                            if 0 <= qv <= 15 and (qv % 4 == 3 or qv == 15):
                                g = qv // 4
                                v.wait_ge(s_pe.h, M[f"attV_{b}_{h}_{qv}"])
                                w = (qv % 4 + 1) * 128
                                ins = v.tensor_copy(
                                    ySB[parity * 64:parity * 64 + 64,
                                        pair * T + g * 512: pair * T + g * 512 + w],
                                    Yp[parity * 64:parity * 64 + 64,
                                       (g % 2) * 512:(g % 2) * 512 + w])
                                s_ve.inc(ins)
                                assert s_ve.n == M[f"yev_{b}_{h}_{g}"]
                            if qt <= 15:
                                v.wait_ge(s_ac.h, M[f"p1_{b}_{h}_{qt}"])
                                if qt >= 2:
                                    v.wait_ge(s_pe.h, M[f"attT_{b}_{h}_{qt-2}_{(qt-2)//4}"])
                                elif h >= 1:
                                    v.wait_ge(s_pe.h, M[f"attT_{b}_{h-1}_{14+qt}_{(14+qt)//4}"])
                                vv = v1[:, (qt % 2) * T:(qt % 2) * T + T]
                                if qt > 0:
                                    v.tensor_tensor(v2[:, 0:qt * 128], vv[:, 0:qt * 128],
                                                    vv[:, 0:qt * 128], ALU.mult)
                                v.tensor_tensor(stat[:, 1:2], mado[:, qt:qt + 1], iccol[:],
                                                ALU.mult)
                                v.tensor_tensor(v2[:, qt * 128:qt * 128 + 128],
                                                vv[:, qt * 128:qt * 128 + 128], maskq[:], ALU.mult)
                                v.tensor_tensor(stat[:, 1:2], stat[:, 1:2], jecol[:], ALU.add)
                                v.tensor_tensor(v2[:, qt * 128:qt * 128 + 128],
                                                v2[:, qt * 128:qt * 128 + 128],
                                                v2[:, qt * 128:qt * 128 + 128], ALU.mult)
                                v.tensor_tensor(stat[:, 2:3], stat[:, 1:2], stat[:, 1:2], ALU.mult)
                                v.scalar_tensor_tensor(
                                    r4[:, (qt % 2) * T:(qt % 2) * T + (qt + 1) * 128],
                                    v2[:, 0:(qt + 1) * 128], 1.0, v2[:, 0:(qt + 1) * 128],
                                    ALU.mult, ALU.mult, accum_out=den16[:, qt:qt + 1])
                                v.scalar_tensor_tensor(stat[:, 3:4], stat[:, 2:3], 1e-6,
                                                       stat[:, 2:3], ALU.mult, ALU.mult)
                                vspace(v)
                                vspace(v)
                                v.tensor_tensor(stat[:, 4:5], den16[:, qt:qt + 1],
                                                stat[:, 3:4], ALU.add)
                                vspace(v)
                                v.reciprocal(inv16[:, qt:qt + 1], stat[:, 4:5])
                                vspace(v)
                                ins = v.tensor_scalar(diag[:, (qt % 2) * 128:(qt % 2) * 128 + 128],
                                                      ident[:], inv16[:, qt:qt + 1], None, ALU.mult)
                                s_ve.inc(ins)
                                assert s_ve.n == M[f"vq_{b}_{h}_{qt}"]
                    blk.vector(f_ve)
                    assert s_ve.n == pred_ve and s_te.n == pred_te

                barrier()

            with nc.sbuf_tensor(f"x0b{b}", [128, 2 * C], F32) as x0b, \
                 nc.sbuf_tensor(f"xpt{b}", [128, 2 * C], F32) as xpt, \
                 nc.sbuf_tensor(f"pscr{b}", [128, C], F32) as pscr, \
                 nc.sbuf_tensor(f"lnc2{b}", [128, 8], F32) as lnc2, \
                 nc.psum_tensor(f"pprj{b}", [128, 2 * C], F32) as pprj:

                for tt in range(NTB):
                    par = tt % 2
                    row = tok0 + tt * 128
                    gt = b * NTB + tt

                    def f_sync(sync):
                        if tt >= 2:
                            sync.wait_ge(s_ve.h, M[f"x0bu_{b}_{tt-2}"])
                        ins = sync.dma_start(out=x0b[:, par * C:(par + 1) * C],
                                             in_=x0[row:row + 128, :])
                        s_sd.inc(ins, 16)
                        M[f"x0r_{b}_{tt}"] = s_sd.n
                    blk.sync(f_sync)

                    def f_ve0(v):
                        v.wait_ge(s_sd.h, M[f"x0r_{b}_{tt}"])
                        ins = v.tensor_tensor(x0b[:, par * C:(par + 1) * C],
                                              x0b[:, par * C:(par + 1) * C], BPROJ[:], ALU.add)
                        s_ve.inc(ins)
                        M[f"x0b_{b}_{tt}"] = s_ve.n
                    blk.vector(f_ve0)

                    def f_pe(t):
                        if tt == 0:
                            t.wait_ge(s_gd.h, M["wproj_ld"])
                        if tt >= 2:
                            t.wait_ge(s_ve.h, M[f"prje_{b}_{tt-2}"])
                        for k in range(8):
                            for hf in range(2):
                                ins = t.matmul(pprj[:, par * C + hf * 512: par * C + (hf + 1) * 512],
                                               ySB[:, k * T + tt * 128: k * T + tt * 128 + 128],
                                               wsb[:, k * C + hf * 512: k * C + (hf + 1) * 512],
                                               start=(k == 0), stop=(k == 7))
                        t.mark_drain(f"prj_{b}_{tt}")
                    TB(f_pe)

                    def f_ve(v):
                        v.wait_ge(s_pe.h, M[f"prj_{b}_{tt}"])
                        v.wait_ge(s_ve.h, M[f"x0b_{b}_{tt}"])
                        if tt >= 2:
                            v.wait_ge(s_sd.h, M[f"xps_{b}_{tt-2}"])
                            v.wait_ge(s_ac.h, M[f"ssq2_{b}_{tt-2}"])
                        ins = v.scalar_tensor_tensor(xpt[:, par * C:(par + 1) * C],
                                                     pprj[:, par * C:(par + 1) * C], 1.0,
                                                     x0b[:, par * C:(par + 1) * C],
                                                     ALU.mult, ALU.add)
                        s_ve.inc(ins)
                        M[f"xp_{b}_{tt}"] = s_ve.n
                        M[f"prje_{b}_{tt}"] = s_ve.n
                        M[f"x0bu_{b}_{tt}"] = s_ve.n
                        vspace(v)
                        v.reduce_sum(lnc2[:, 0:1], xpt[:, par * C:(par + 1) * C], AXX)
                        v.reduce_max(lnc2[:, 2:3], xpt[:, par * C:(par + 1) * C], AXX)
                        ins = v.tensor_reduce(lnc2[:, 3:4], xpt[:, par * C:(par + 1) * C],
                                              AXX, ALU.min)
                        s_ve.inc(ins)
                        M[f"ln2r_{b}_{tt}"] = s_ve.n
                    blk.vector(f_ve)

                    def f_sync2(sync):
                        sync.wait_ge(s_ve.h, M[f"xp_{b}_{tt}"])
                        ins = sync.dma_start(out=xp[row:row + 128, :],
                                             in_=xpt[:, par * C:(par + 1) * C])
                        s_sd.inc(ins, 16)
                        M[f"xps_{b}_{tt}"] = s_sd.n
                    blk.sync(f_sync2)

                    def f_sc(sc):
                        sc.wait_ge(s_ve.h, M[f"ln2r_{b}_{tt}"])
                        sc.activation(pscr[:], xpt[:, par * C:(par + 1) * C], AF.Square,
                                      accum_out=lnc2[:, 1:2])
                        sc_mark(sc, f"ssq2_{b}_{tt}")
                    blk.scalar(f_sc)

                    def f_ve2(v):
                        v.wait_ge(s_ac.h, M[f"ssq2_{b}_{tt}"])
                        v.tensor_scalar(lnc2[:, 4:5], lnc2[:, 0:1], 1.0 / C, None, ALU.mult)
                        vspace(v)
                        v.tensor_tensor(lnc2[:, 5:6], lnc2[:, 4:5], lnc2[:, 4:5], ALU.mult)
                        vspace(v)
                        v.scalar_tensor_tensor(lnc2[:, 7:8], lnc2[:, 1:2], 1.0 / C,
                                               lnc2[:, 5:6], ALU.mult, ALU.subtract)
                        vspace(v)
                        ins = v.tensor_scalar(lnc2[:, 7:8], lnc2[:, 7:8], 1e-5, None, ALU.add)
                        s_ve.inc(ins)
                        M[f"var2_{b}_{tt}"] = s_ve.n
                    blk.vector(f_ve2)

                    def f_sc2(sc):
                        sc.wait_ge(s_ve.h, M[f"var2_{b}_{tt}"])
                        ins = sc.activation(lnc2[:, 6:7], lnc2[:, 7:8], AF.Sqrt)
                        s_ac.inc(ins)
                        M[f"sqv2_{b}_{tt}"] = s_ac.n
                    blk.scalar(f_sc2)

                    def f_ve3(v):
                        v.wait_ge(s_ac.h, M[f"sqv2_{b}_{tt}"])
                        v.reciprocal(rs2[:, gt:gt + 1], lnc2[:, 6:7])
                        v.tensor_tensor(lnc2[:, 2:3], lnc2[:, 2:3], lnc2[:, 4:5], ALU.subtract)
                        v.tensor_scalar(mb2[:, gt:gt + 1], lnc2[:, 4:5], rs2[:, gt:gt + 1],
                                        -1.0, ALU.mult, ALU.mult)
                        v.tensor_tensor(lnc2[:, 3:4], lnc2[:, 4:5], lnc2[:, 3:4], ALU.subtract)
                        vspace(v)
                        v.tensor_tensor(lnc2[:, 2:3], lnc2[:, 2:3], lnc2[:, 3:4], ALU.max)
                        vspace(v)
                        ins = v.tensor_scalar(lmx[:, gt:gt + 1], lnc2[:, 2:3],
                                              rs2[:, gt:gt + 1], None, ALU.mult)
                        s_ve.inc(ins)
                        M[f"lmx_{b}_{tt}"] = s_ve.n
                    blk.vector(f_ve3)

                barrier()

        with nc.psum_tensor("ptx", [32, 128], F32) as ptx, \
             nc.psum_tensor("pbx", [128, C + 512], F32) as pbx:

            def p4_ve(v):
                vspace(v)
                vspace(v)
                ins = v.reduce_max(gmxc[:], lmx[:], AXX)
                s_ve.inc(ins)
                M["gmxcol"] = s_ve.n
            blk.vector(p4_ve)

            def p4_pe(t):
                t.wait_ge(s_ve.h, M["gmxcol"])
                ins = t.transpose(ptx[0:1, :], gmxc[:, 0:1], identf[:])
                t.mark_drain("gmxT")
            TB(p4_pe)

            def p4_ve2(v):
                v.wait_ge(s_pe.h, M["gmxT"])
                v.reduce_max(ccb[0:1, 0:1], ptx[0:1, 0:128], AXX)
                vspace(v)
                vspace(v)
                for j in range(1, 8):
                    ins = v.tensor_copy(ccb[0:1, j:j + 1], ccb[0:1, 0:1])
                s_ve.inc(ins)
                M["ccb"] = s_ve.n
            blk.vector(p4_ve2)

            def p4_sync(sync):
                sync.wait_ge(s_ve.h, M["ccb"])
                ins = sync.dma_start(out=cc_in[0:1, 0:8], in_=ccb[:])
                s_sd.inc(ins, 16)
                M["ccpush"] = s_sd.n
            blk.sync(p4_sync)

            def p4_gp(gp):
                gp.wait_ge(s_sd.h, M["ccpush"])
                ins = gp.collective_compute(
                    "AllReduce", ALU.max,
                    replica_groups=[[0, 1, 2, 3, 4, 5, 6, 7]],
                    ins=[cc_in[0:1, 0:8]],
                    outs=[cc_out[0:1, 0:8]],
                )
                s_cc.inc(ins)
                M["ccdone"] = s_cc.n
            blk.gpsimd(p4_gp)

            def p4_sync2(sync):
                sync.wait_ge(s_cc.h, M["ccdone"])
                ins = sync.dma_start(out=ccb[:], in_=cc_out[0:1, 0:8])
                s_sd.inc(ins, 16)
                M["ccpull"] = s_sd.n
            blk.sync(p4_sync2)

            def p4_ve3(v):
                v.wait_ge(s_sd.h, M["ccpull"])
                v.tensor_scalar(stat[0:1, 5:6], ccb[0:1, 0:1], 1.0 / 127.0, None, ALU.mult)
                vspace(v)
                v.tensor_scalar(stat[0:1, 5:6], stat[0:1, 5:6], 1e-5, None, ALU.max)
                vspace(v)
                v.reciprocal(stat[0:1, 6:7], stat[0:1, 5:6])
                vspace(v)
                ins = v.tensor_scalar(wsrw[:], wsrw[:], stat[0:1, 5:6], None, ALU.mult)
                s_ve.inc(ins)
                M["sx"] = s_ve.n
            blk.vector(p4_ve3)

            def p4_pe2(t):
                t.wait_ge(s_ve.h, M["sx"])
                for i in range(2):
                    t.matmul(pbx[:, i * 512:(i + 1) * 512], onescol[:],
                             wsrw[0:1, i * 512:(i + 1) * 512], start=True, stop=True)
                ins = t.matmul(pbx[:, C:C + 1], onescol[:], stat[0:1, 6:7],
                               start=True, stop=True)
                t.mark_drain("fscmat")
            TB(p4_pe2)

            def p4_ve4(v):
                v.wait_ge(s_pe.h, M["fscmat"])
                v.tensor_copy(FSC[:], pbx[:, 0:C])
                ins = v.tensor_copy(isxcol[:], pbx[:, C:C + 1])
                s_ve.inc(ins)
                M["fsc"] = s_ve.n
            blk.vector(p4_ve4)

        def p4_gpw(gp):
            ins = gp.dma_start(out=w8stg[:], in_=w8t[0])
            s_gd.inc(ins, 16)
            M["w8ld_0"] = s_gd.n
        blk.gpsimd(p4_gpw)

        for k in range(8):
            def f_w8v(v):
                v.wait_ge(s_gd.h, M[f"w8ld_{k}"])
                v.tensor_copy(wsb[:, k * C:(k + 1) * C], w8stg[:])
                ins = v.tensor_copy(vjk[:, 0:1], vjk[:, 1:2])
                s_ve.inc(ins)
                M[f"w8c_{k}"] = s_ve.n
            blk.vector(f_w8v)
            if k + 1 < 8:
                def f_w8g(gp):
                    gp.wait_ge(s_ve.h, M[f"w8c_{k}"])
                    ins = gp.dma_start(out=w8stg[:], in_=w8t[k + 1])
                    s_gd.inc(ins, 16)
                    M[f"w8ld_{k+1}"] = s_gd.n
                blk.gpsimd(f_w8g)

        barrier()

        with nc.sbuf_tensor("xpr", [128, 2 * C], F32) as xpr, \
             nc.sbuf_tensor("ln2s", [128, C], F32) as ln2s, \
             nc.sbuf_tensor("q32", [128, C], I32) as q32, \
             nc.sbuf_tensor("x8", [128, 2 * C], BF16) as x8, \
             nc.sbuf_tensor("x8T", [128, 2 * C], BF16) as x8T, \
             nc.sbuf_tensor("ffs", [128, 2 * C], F32) as ffs, \
             nc.psum_tensor("ptf", [128, 2 * C], BF16) as ptf, \
             nc.psum_tensor("pff", [128, 2 * C], F32) as pff:

            for gt in range(NT):
                par = gt % 2
                row = gt * 128

                def f_sync(sync):
                    if gt >= 2:
                        sync.wait_ge(s_ve.h, M[f"ffe_{gt-2}"])
                    ins = sync.dma_start(out=xpr[:, par * C:(par + 1) * C],
                                         in_=xp[row:row + 128, :])
                    s_sd.inc(ins, 16)
                    M[f"xpl_{gt}"] = s_sd.n
                blk.sync(f_sync)

                def f_sc(sc):
                    sc.wait_ge(s_sd.h, M[f"xpl_{gt}"])
                    if gt >= 1:
                        sc.wait_ge(s_ve.h, M[f"q32_{gt-1}"])
                    ins = sc.activation(ln2s[:],
                                        xpr[:, par * C:(par + 1) * C], AF.Identity,
                                        bias=mb2[:, gt:gt + 1], scale=rs2[:, gt:gt + 1])
                    s_ac.inc(ins)
                    M[f"ln2_{gt}"] = s_ac.n
                blk.scalar(f_sc)

                def f_ve(v):
                    v.wait_ge(s_ac.h, M[f"ln2_{gt}"])
                    if gt == 0:
                        v.wait_ge(s_ve.h, M["fsc"])
                    if gt >= 2:
                        v.wait_ge(s_pe.h, M[f"x8tr_{gt-2}"])
                    v.tensor_scalar(q32[:], ln2s[:], isxcol[:], 0.0,
                                    ALU.mult, ALU.add)
                    vspace(v)
                    ins = v.tensor_copy(x8[:, par * C:(par + 1) * C], q32[:])
                    s_ve.inc(ins)
                    M[f"q32_{gt}"] = s_ve.n
                blk.vector(f_ve)

                def f_pe(t):
                    t.wait_ge(s_ve.h, M[f"q32_{gt}"])
                    if gt >= 2:
                        t.wait_ge(s_ve.h, M[f"x8e_{gt-2}"])
                    for c in range(8):
                        ins = t.transpose(ptf[:, par * C + c * 128: par * C + (c + 1) * 128],
                                          x8[:, par * C + c * 128: par * C + (c + 1) * 128],
                                          ident[:])
                    t.mark_drain(f"x8tr_{gt}")
                TB(f_pe)

                def f_ve2(v):
                    v.wait_ge(s_pe.h, M[f"x8tr_{gt}"])
                    if gt >= 2:
                        v.wait_ge(s_pe.h, M[f"ffmm_{gt-2}"])
                    ins = v.tensor_copy(x8T[:, par * C:(par + 1) * C],
                                        ptf[:, par * C:(par + 1) * C])
                    s_ve.inc(ins)
                    M[f"x8e_{gt}"] = s_ve.n
                blk.vector(f_ve2)

                def f_pe2(t):
                    t.wait_ge(s_ve.h, M[f"x8e_{gt}"])
                    if gt == 0:
                        t.wait_ge(s_ve.h, M["w8c_7"])
                    if gt >= 2:
                        t.wait_ge(s_ve.h, M[f"ffe_{gt-2}"])
                    for k in range(8):
                        for hf in range(2):
                            ins = t.matmul(pff[:, par * C + hf * 512: par * C + (hf + 1) * 512],
                                           x8T[:, par * C + k * 128: par * C + (k + 1) * 128],
                                           wsb[:, k * C + hf * 512: k * C + (hf + 1) * 512],
                                           start=(k == 0), stop=(k == 7))
                    t.mark_drain(f"ffmm_{gt}")
                TB(f_pe2)

                def f_gp(gp):
                    gp.wait_ge(s_sd.h, M[f"xpl_{gt}"])
                    if gt >= 2:
                        gp.wait_ge(s_st.h, M[f"outs_{gt-2}"])
                    ins = gp.tensor_tensor(ffs[:, par * C:(par + 1) * C],
                                           xpr[:, par * C:(par + 1) * C], FBX[:], ALU.add)
                    s_gp.inc(ins)
                    M[f"fbx_{gt}"] = s_gp.n
                blk.gpsimd(f_gp)

                def f_ve3(v):
                    v.wait_ge(s_pe.h, M[f"ffmm_{gt}"])
                    v.wait_ge(s_gp.h, M[f"fbx_{gt}"])
                    if gt >= 2:
                        v.wait_ge(s_st.h, M[f"outs_{gt-2}"])
                    v.tensor_tensor(BPROJ[:], pff[:, par * C:(par + 1) * C], FSC[:], ALU.mult)
                    vspace(v)
                    ins = v.tensor_tensor(ffs[:, par * C:(par + 1) * C],
                                          ffs[:, par * C:(par + 1) * C], BPROJ[:], ALU.add)
                    s_ve.inc(ins)
                    M[f"ffe_{gt}"] = s_ve.n
                blk.vector(f_ve3)

                def f_sync2(sync):
                    sync.wait_ge(s_ve.h, M[f"ffe_{gt}"])
                    ins = sync.dma_start(out=out[row:row + 128, :],
                                         in_=ffs[:, par * C:(par + 1) * C])
                    s_st.inc(ins, 16)
                    M[f"outs_{gt}"] = s_st.n
                blk.sync(f_sync2)

            def f_fin(sync):
                sync.wait_ge(s_st.h, s_st.n)
            blk.sync(f_fin)

    return nc


_NC_CACHE = {}


def kernel(L0_attn_x, L0_attn_W_attn, L0_attn_b_attn, L0_attn_W_proj, L0_attn_b_proj,
           L0_attn_score_gain, L0_ffn_weight_int8, L0_ffn_weight_scale, L0_ffn_bias):
    x = np.ascontiguousarray(np.asarray(L0_attn_x, dtype=np.float32))
    W = np.asarray(L0_attn_W_attn, dtype=np.float32)
    ba = np.ascontiguousarray(np.asarray(L0_attn_b_attn, dtype=np.float32).reshape(3 * C, 1))
    Wp = np.asarray(L0_attn_W_proj, dtype=np.float32)
    bp = np.ascontiguousarray(np.asarray(L0_attn_b_proj, dtype=np.float32).reshape(1, C))
    g = np.ascontiguousarray(np.asarray(L0_attn_score_gain, dtype=np.float32).reshape(1, 1))
    w8 = np.asarray(L0_ffn_weight_int8)
    ws = np.ascontiguousarray(np.asarray(L0_ffn_weight_scale, dtype=np.float32).reshape(1, C))
    fb = np.ascontiguousarray(np.asarray(L0_ffn_bias, dtype=np.float32).reshape(1, C))

    wattn_m = np.ascontiguousarray(
        W.reshape(8, 128, 24, 128).transpose(2, 1, 0, 3).reshape(24, 128, C))
    wproj_m = np.ascontiguousarray(Wp.reshape(8, 128, C))
    w8t = np.ascontiguousarray(w8.T.reshape(8, 128, C))

    if "nc" not in _NC_CACHE:
        _NC_CACHE["nc"] = build()
    nc = _NC_CACHE["nc"]

    in_maps = []
    for c in range(8):
        in_maps.append({
            "x0": np.ascontiguousarray(x[2 * c:2 * c + 2].reshape(NTOK, C)),
            "wattn_m": wattn_m, "battn": ba, "wproj_m": wproj_m, "bprow": bp,
            "grow": g, "w8t": w8t, "wsrow": ws, "fbrow": fb,
        })
    res = run_bass_kernel_spmd(nc, in_maps, core_ids=list(range(8)))
    outs = [np.asarray(res.results[c]["out"]).reshape(2, T, C) for c in range(8)]
    return np.concatenate(outs, axis=0).astype(np.float32)


# version 22 (final)
# speedup vs baseline: 1.3068x; 1.3068x over previous
import numpy as np
import concourse.bass as bass
import concourse.mybir as mybir
from concourse.bass_utils import run_bass_kernel_spmd
from contextlib import ExitStack

F32 = mybir.dt.float32
BF16 = mybir.dt.bfloat16
I8 = mybir.dt.int8
I32 = mybir.dt.int32
AF = mybir.ActivationFunctionType
ALU = mybir.AluOpType
AXX = mybir.AxisListType.X

DT16 = BF16
T = 2048
C = 1024
H = 16
D = 64
BL = 2
NTOK = BL * T
NT = NTOK // 128
NTB = T // 128
QT = 16


class Sem:
    def __init__(self, nc, ctx, name):
        self.h = ctx.enter_context(nc.semaphore(name))
        self.n = 0

    def inc(self, instr, k=1):
        instr.then_inc(self.h, k)
        self.n += k
        return self.n


def build():
    nc = bass.Bass("TRN2", target_bir_lowering=False, detect_race_conditions=False)

    x0 = nc.dram_tensor("x0", [NTOK, C], F32, kind="ExternalInput").ap()
    wattn_m = nc.dram_tensor("wattn_m", [24, 128, C], F32, kind="ExternalInput").ap()
    battn = nc.dram_tensor("battn", [3 * C, 1], F32, kind="ExternalInput").ap()
    wproj_m = nc.dram_tensor("wproj_m", [8, 128, C], F32, kind="ExternalInput").ap()
    bprow = nc.dram_tensor("bprow", [1, C], F32, kind="ExternalInput").ap()
    grow = nc.dram_tensor("grow", [1, 1], F32, kind="ExternalInput").ap()
    w8t = nc.dram_tensor("w8t", [8, 128, C], I8, kind="ExternalInput").ap()
    wsrow = nc.dram_tensor("wsrow", [1, C], F32, kind="ExternalInput").ap()
    fbrow = nc.dram_tensor("fbrow", [1, C], F32, kind="ExternalInput").ap()
    out = nc.dram_tensor("out", [NTOK, C], F32, kind="ExternalOutput").ap()
    xp = nc.dram_tensor("xp", [NTOK, C], F32, kind="Internal").ap()
    cc_in = nc.dram_tensor("cc_in", [1, 8], F32, kind="Internal").ap()
    cc_out = nc.dram_tensor("cc_out", [1, 8], F32, kind="Internal").ap()

    ctx = ExitStack()
    with ctx:
        def sb(name, shape, dt=F32):
            return ctx.enter_context(nc.sbuf_tensor(name, shape, dt))

        qT = sb("qT", [128, 8 * T], DT16)
        kT = sb("kT", [128, 8 * T], DT16)
        VT = sb("VT", [128, NTB * C], DT16)
        ySB = sb("ySB", [128, 8 * T], DT16)
        wsb = sb("wsb", [128, 8 * C], DT16)
        ident = sb("ident", [128, 128], DT16)
        identf = sb("identf", [128, 128], F32)
        maskq = sb("maskq", [128, 128], DT16)
        battn16 = sb("battn16", [128, 24])
        sgb8 = sb("sgb8", [128, 8])
        onescol = sb("onescol", [1, 128])
        grw = sb("grw", [1, 8])
        sgcol = sb("sgcol", [128, 1])
        iccol = sb("iccol", [128, 1])
        jecol = sb("jecol", [128, 1])
        isxcol = sb("isxcol", [128, 1])
        bprw = sb("bprw", [1, C])
        wsrw = sb("wsrw", [1, C])
        fbrw = sb("fbrw", [1, C])
        ccb = sb("ccb", [1, 8])
        BPROJ = sb("BPROJ", [128, C])
        FBX = sb("FBX", [128, C])
        FSC = sb("FSC", [128, C])
        mb2 = sb("mb2", [128, NT])
        rs2 = sb("rs2", [128, NT])
        lmx = sb("lmx", [128, NT])
        gmxc = sb("gmxc", [128, 1])
        mado = sb("mado", [128, QT])
        den16 = sb("den16", [128, QT])
        inv16 = sb("inv16", [128, QT])
        stat = sb("stat", [128, 8])
        diag = sb("diag", [128, 2 * 128], BF16)
        w8stg = sb("w8stg", [128, C], I8)
        mnsb = sb("mnsb", [128, NT])

        s_sd = Sem(nc, ctx, "s_sd")
        s_gd = Sem(nc, ctx, "s_gd")
        s_pe = Sem(nc, ctx, "s_pe")
        s_ac = Sem(nc, ctx, "s_ac")
        s_ve = Sem(nc, ctx, "s_ve")
        s_gp = Sem(nc, ctx, "s_gp")
        s_st = Sem(nc, ctx, "s_st")
        s_cc = Sem(nc, ctx, "s_cc")
        s_ph = Sem(nc, ctx, "s_ph")
        s_te = Sem(nc, ctx, "s_te")
        s_xd = Sem(nc, ctx, "s_xd")
        M = {}
        scsc = sb("scsc", [128, 2])
        vjk = sb("vjk", [128, 2])

        blk = ctx.enter_context(nc.Block())

        pe_pend = [0]

        class PeProxy:
            def __init__(self, t):
                self.t = t

            def _att(self, ins):
                if pe_pend[0]:
                    ins.then_inc(s_pe.h, pe_pend[0])
                    s_pe.n += pe_pend[0]
                    pe_pend[0] = 0
                return ins

            def matmul(self, *a, **k):
                return self._att(self.t.matmul(*a, **k))

            def transpose(self, *a, **k):
                return self._att(self.t.transpose(*a, **k))

            def wait_ge(self, *a):
                return self.t.wait_ge(*a)

            def sem_inc(self, *a):
                return self.t.sem_inc(*a)

            def drain(self):
                return self.t.drain()

            def mark(self, key):
                pe_pend[0] += 1
                M[key] = s_pe.n + pe_pend[0]

            def mark_drain(self, key):
                ins = self.t.drain()
                k = pe_pend[0] + 1
                ins.then_inc(s_pe.h, k)
                s_pe.n += k
                pe_pend[0] = 0
                M[key] = s_pe.n

            def flush(self):
                if pe_pend[0]:
                    ins = self.t.drain()
                    ins.then_inc(s_pe.h, pe_pend[0])
                    s_pe.n += pe_pend[0]
                    pe_pend[0] = 0

        def TB(f):
            blk.tensor(lambda t: f(PeProxy(t)))

        def sc_mark(sc, key):
            ins = sc.copy(scsc[:, 0:1], scsc[:, 1:2])
            s_ac.inc(ins)
            M[key] = s_ac.n

        def vspace(v):
            v.tensor_copy(vjk[:, 0:1], vjk[:, 1:2])

        def barrier():
            tgt = s_ph.n + 5

            def _t(e):
                e.sem_inc(s_ph.h, 1)
                e.wait_ge(s_ph.h, tgt)

            def _tpe(t):
                PeProxy(t).flush()
                _t(t)
            blk.tensor(_tpe)
            blk.scalar(_t)
            blk.vector(_t)
            blk.gpsimd(_t)
            blk.sync(_t)
            s_ph.n = tgt

        def ph0_sync(sync):
            ins = sync.dma_start(out=battn16[:],
                                 in_=battn.rearrange("(a b) c -> b (a c)", b=128),
                                 allow_slow_non_contiguous=True)
            s_sd.inc(ins, 16)
            for ap_out, ap_in in [
                (grw[0:1, 0:1], grow), (bprw[:], bprow),
                (wsrw[:], wsrow), (fbrw[:], fbrow)]:
                ins = sync.dma_start(out=ap_out, in_=ap_in)
                s_sd.inc(ins, 16)
            M["ph0_sd"] = s_sd.n
        blk.sync(ph0_sync)

        def ph0_gp(gp):
            for tile, fillv, cmp_op, selfill in [
                    (ident, 0.0, ALU.not_equal, 1.0),
                    (identf, 0.0, ALU.not_equal, 1.0),
                    (maskq, 1.0, ALU.is_ge, 0.0)]:
                ins = gp.memset(tile[:], fillv)
                s_gp.inc(ins)
                gp.wait_ge(s_gp.h, s_gp.n)
                ins = gp.affine_select(out=tile[:], in_=tile[:], compare_op=cmp_op,
                                       fill=selfill, base=0, pattern=[[-1, 128]],
                                       channel_multiplier=1)
                s_gp.inc(ins)
                gp.wait_ge(s_gp.h, s_gp.n)
            gp.memset(vjk[:], 0.0)
            gp.memset(scsc[:], 0.0)
            ins = gp.memset(onescol[:], 1.0)
            s_gp.inc(ins)
            M["ph0_gp"] = s_gp.n
        blk.gpsimd(ph0_gp)

        def ph0_vec(v):
            v.wait_ge(s_sd.h, M["ph0_sd"])
            v.wait_ge(s_gp.h, M["ph0_gp"])
            v.tensor_scalar(grw[0:1, 1:2], grw[0:1, 0:1], -1.0, None, ALU.mult)
            vspace(v)
            vspace(v)
            v.tensor_tensor(grw[0:1, 2:3], grw[0:1, 0:1], grw[0:1, 1:2], ALU.max)
            vspace(v)
            vspace(v)
            v.tensor_scalar(grw[0:1, 3:4], grw[0:1, 2:3], 1e-30, None, ALU.max)
            vspace(v)
            vspace(v)
            v.tensor_scalar(grw[0:1, 4:5], grw[0:1, 3:4], 1024.0, None, ALU.mult)
            vspace(v)
            vspace(v)
            v.reciprocal(grw[0:1, 5:6], grw[0:1, 4:5])
            v.reciprocal(grw[0:1, 6:7], grw[0:1, 3:4])
            vspace(v)
            vspace(v)
            ins = v.tensor_scalar(grw[0:1, 7:8], grw[0:1, 6:7], 1e-6, None, ALU.mult)
            s_ve.inc(ins)
            M["ph0_g"] = s_ve.n
        blk.vector(ph0_vec)

        def ph0_sc(sc):
            sc.wait_ge(s_ve.h, M["ph0_g"])
            ins = sc.sign(grw[0:1, 0:1], grw[0:1, 0:1])
            s_ac.inc(ins)
            M["ph0_sg"] = s_ac.n
        blk.scalar(ph0_sc)

        with nc.psum_tensor("pbc", [128, 4], F32) as pbc, \
             nc.psum_tensor("pmat", [128, 2 * C], F32) as pmat:

            def ph0_pe(t):
                t.wait_ge(s_ac.h, M["ph0_sg"])
                t.matmul(pbc[:, 0:1], onescol[:], grw[0:1, 0:1], start=True, stop=True)
                t.matmul(pbc[:, 1:2], onescol[:], grw[0:1, 5:6], start=True, stop=True)
                t.matmul(pbc[:, 2:3], onescol[:], grw[0:1, 7:8], start=True, stop=True)
                for i in range(2):
                    t.matmul(pmat[:, i * 512:(i + 1) * 512], onescol[:],
                             bprw[0:1, i * 512:(i + 1) * 512], start=True, stop=True)
                    ins = t.matmul(pmat[:, C + i * 512:C + (i + 1) * 512], onescol[:],
                                   fbrw[0:1, i * 512:(i + 1) * 512], start=True, stop=True)
                t.mark_drain("ph0_mat")
            TB(ph0_pe)

            def ph0_vec2(v):
                v.wait_ge(s_pe.h, M["ph0_mat"])
                v.tensor_copy(sgcol[:], pbc[:, 0:1])
                v.tensor_copy(iccol[:], pbc[:, 1:2])
                v.tensor_copy(jecol[:], pbc[:, 2:3])
                v.tensor_copy(BPROJ[:], pmat[:, 0:C])
                v.tensor_copy(FBX[:], pmat[:, C:2 * C])
                ins = v.tensor_scalar(sgb8[:], battn16[:, 0:8], sgcol[:], 0.0,
                                      ALU.mult, ALU.add)
                s_ve.inc(ins)
                M["ph0_cols"] = s_ve.n
            blk.vector(ph0_vec2)

        def ph0_gp2(gp):
            for m in range(8):
                ins = gp.dma_start(out=wsb[:, m * C:(m + 1) * C], in_=wproj_m[m])
                s_gd.inc(ins, 16)
            M["wproj_ld"] = s_gd.n
        blk.gpsimd(ph0_gp2)

        barrier()

        ySB3 = ySB[:].rearrange("p (c n) -> p c n", n=T)

        for b in range(BL):
            tok0 = b * T

            with nc.sbuf_tensor(f"x0t{b}", [128, 2 * C], F32) as x0t, \
                 nc.sbuf_tensor(f"lnscr{b}", [128, C], F32) as lnscr, \
                 nc.sbuf_tensor(f"ln1t{b}", [128, 2 * C], DT16) as ln1t, \
                 nc.sbuf_tensor(f"lncols{b}", [128, 8], F32) as lncols, \
                 nc.psum_tensor(f"ptr1{b}", [128, 2 * C], DT16) as ptr1:

                for tt in range(NTB):
                    par = tt % 2
                    row = tok0 + tt * 128

                    def f_sync(sync):
                        if tt >= 2:
                            sync.wait_ge(s_ac.h, M[f"ln1n_{b}_{tt-2}"])
                        ins = sync.dma_start(out=x0t[:, par * C:(par + 1) * C],
                                             in_=x0[row:row + 128, :])
                        s_sd.inc(ins, 16)
                        M[f"x0_{b}_{tt}"] = s_sd.n
                    blk.sync(f_sync)

                    def f_sc1(sc):
                        sc.wait_ge(s_sd.h, M[f"x0_{b}_{tt}"])
                        sc.activation(lnscr[:], x0t[:, par * C:(par + 1) * C],
                                      AF.Square, accum_out=lncols[:, 1:2])
                        sc_mark(sc, f"ssq_{b}_{tt}")
                    blk.scalar(f_sc1)

                    def f_ve1(v):
                        v.wait_ge(s_sd.h, M[f"x0_{b}_{tt}"])
                        v.reduce_sum(lncols[:, 0:1], x0t[:, par * C:(par + 1) * C], AXX)
                        v.wait_ge(s_ac.h, M[f"ssq_{b}_{tt}"])
                        vspace(v)
                        v.tensor_scalar(lncols[:, 2:3], lncols[:, 0:1], 1.0 / C, None, ALU.mult)
                        vspace(v)
                        v.tensor_tensor(lncols[:, 3:4], lncols[:, 2:3], lncols[:, 2:3], ALU.mult)
                        vspace(v)
                        v.scalar_tensor_tensor(lncols[:, 4:5], lncols[:, 1:2], 1.0 / C,
                                               lncols[:, 3:4], ALU.mult, ALU.subtract)
                        vspace(v)
                        ins = v.tensor_scalar(lncols[:, 4:5], lncols[:, 4:5], 1e-5, None, ALU.add)
                        s_ve.inc(ins)
                        M[f"var_{b}_{tt}"] = s_ve.n
                    blk.vector(f_ve1)

                    def f_sc2(sc):
                        sc.wait_ge(s_ve.h, M[f"var_{b}_{tt}"])
                        ins = sc.activation(lncols[:, 5:6], lncols[:, 4:5], AF.Sqrt)
                        s_ac.inc(ins)
                        M[f"sqv_{b}_{tt}"] = s_ac.n
                    blk.scalar(f_sc2)

                    def f_ve2(v):
                        v.wait_ge(s_ac.h, M[f"sqv_{b}_{tt}"])
                        v.reciprocal(lncols[:, 6:7], lncols[:, 5:6])
                        vspace(v)
                        ins = v.tensor_scalar(lncols[:, 7:8], lncols[:, 2:3],
                                              lncols[:, 6:7], -1.0, ALU.mult, ALU.mult)
                        s_ve.inc(ins)
                        M[f"rstd_{b}_{tt}"] = s_ve.n
                    blk.vector(f_ve2)

                    def f_sc3(sc):
                        sc.wait_ge(s_ve.h, M[f"rstd_{b}_{tt}"])
                        if tt >= 2:
                            sc.wait_ge(s_pe.h, M[f"ln1tr_{b}_{tt-2}"])
                        sc.activation(ln1t[:, par * C:(par + 1) * C],
                                      x0t[:, par * C:(par + 1) * C], AF.Identity,
                                      bias=lncols[:, 7:8], scale=lncols[:, 6:7])
                        sc_mark(sc, f"ln1n_{b}_{tt}")
                    blk.scalar(f_sc3)

                    def f_pe(t):
                        t.wait_ge(s_ac.h, M[f"ln1n_{b}_{tt}"])
                        if tt >= 2:
                            t.wait_ge(s_ve.h, M[f"ln1e_{b}_{tt-2}"])
                        for c in range(8):
                            ins = t.transpose(ptr1[:, par * C + c * 128: par * C + (c + 1) * 128],
                                              ln1t[:, par * C + c * 128: par * C + (c + 1) * 128],
                                              ident[:])
                        t.mark_drain(f"ln1tr_{b}_{tt}")
                    TB(f_pe)

                    def f_ve3(v):
                        v.wait_ge(s_pe.h, M[f"ln1tr_{b}_{tt}"])
                        src = ptr1[:, par * C:(par + 1) * C].rearrange("p (c n) -> p c n", n=128)
                        dst = ySB3[:, :, tt * 128:(tt + 1) * 128]
                        ins = v.tensor_copy(dst, src)
                        s_ve.inc(ins)
                        M[f"ln1e_{b}_{tt}"] = s_ve.n
                    blk.vector(f_ve3)

                M[f"ln1done_{b}"] = s_ve.n

            with nc.sbuf_tensor(f"wcol{b}", [128, 2 * C], DT16) as wcol, \
                 nc.sbuf_tensor(f"kraw{b}", [128, T], F32) as kraw, \
                 nc.sbuf_tensor(f"vF{b}", [128, 2 * T], DT16) as vF, \
                 nc.psum_tensor(f"pqk{b}", [128, T], F32) as pqk, \
                 nc.psum_tensor(f"ptrv{b}", [128, 2 * C], DT16) as ptrv:

                def emit_vtrans(t, mc):
                    mslot = (16 + mc) % 2
                    if mc >= 1:
                        t.wait_ge(s_ve.h, M[f"vtre_{b}_{mc-1}"])
                    for tb in range(NTB):
                        ins = t.transpose(
                            ptrv[:, (tb // 8) * C + (tb % 8) * 128:(tb // 8) * C + (tb % 8) * 128 + 128],
                            vF[:, mslot * T + tb * 128: mslot * T + tb * 128 + 128],
                            ident[:])
                        if tb == 7:
                            t.mark(f"vtr_{b}_{mc}_0")
                        elif tb == 15:
                            t.mark_drain(f"vtr_{b}_{mc}_1")

                def emit_vtrans_evac(v, mc):
                    for half in range(2):
                        v.wait_ge(s_pe.h, M[f"vtr_{b}_{mc}_{half}"])
                        for tb in range(half * 8, half * 8 + 8):
                            ins = v.tensor_copy(
                                VT[:, tb * C + mc * 128: tb * C + mc * 128 + 128],
                                ptrv[:, (tb // 8) * C + (tb % 8) * 128:(tb // 8) * C + (tb % 8) * 128 + 128])
                        s_ve.inc(ins)
                    M[f"vtre_{b}_{mc}"] = s_ve.n

                for m in range(24):
                    wpar = m % 2

                    def f_gp(gp):
                        if m >= 2:
                            gp.wait_ge(s_pe.h, M[f"qkvmm_{b}_{m-2}"])
                        else:
                            gp.wait_ge(s_ve.h, M[f"ln1done_{b}"])
                        ins = gp.dma_start(out=wcol[:, wpar * C:(wpar + 1) * C], in_=wattn_m[m])
                        s_gd.inc(ins, 16)
                        M[f"wc_{b}_{m}"] = s_gd.n
                    blk.gpsimd(f_gp)

                    def f_pe(t):
                        t.wait_ge(s_gd.h, M[f"wc_{b}_{m}"])
                        if m == 0:
                            t.wait_ge(s_ve.h, M[f"ln1done_{b}"])
                        if m >= 1:
                            t.wait_ge(s_ac.h, M[f"qkve_{b}_{m-1}"])
                        for k in range(8):
                            for hf in range(4):
                                ins = t.matmul(pqk[:, hf * 512:(hf + 1) * 512],
                                               wcol[:, wpar * C + k * 128: wpar * C + (k + 1) * 128],
                                               ySB3[:, k, hf * 512:(hf + 1) * 512],
                                               start=(k == 0), stop=(k == 7))
                        t.mark_drain(f"qkvmm_{b}_{m}")
                        if m - 1 >= 16:
                            emit_vtrans(t, m - 1 - 16)
                    TB(f_pe)

                    def f_sc(sc):
                        sc.wait_ge(s_pe.h, M[f"qkvmm_{b}_{m}"])
                        if m < 8:
                            ins = sc.activation(qT[:, m * T:(m + 1) * T], pqk[:], AF.Identity,
                                                bias=sgb8[:, m:m + 1], scale=sgcol[:])
                        elif m < 16:
                            if m >= 9:
                                sc.wait_ge(s_ve.h, M[f"kTd_{b}_{m-1}"])
                            ins = sc.activation(kraw[:], pqk[:], AF.Identity,
                                                bias=battn16[:, m:m + 1], scale=1.0)
                        else:
                            if m >= 18:
                                sc.wait_ge(s_pe.h, M[f"vtr_{b}_{m-2-16}_{1}"])
                            ins = sc.activation(vF[:, (m % 2) * T:(m % 2) * T + T], pqk[:],
                                                AF.Identity, bias=battn16[:, m:m + 1], scale=1.0)
                        sc_mark(sc, f"qkve_{b}_{m}")
                    blk.scalar(f_sc)

                    if 8 <= m < 16:
                        def f_ve(v):
                            v.wait_ge(s_ac.h, M[f"qkve_{b}_{m}"])
                            v.reduce_sum(stat[:, 0:1], kraw[:], AXX)
                            vspace(v)
                            vspace(v)
                            v.tensor_scalar(stat[:, 2:3], stat[:, 0:1], 1.0 / T, None, ALU.mult)
                            vspace(v)
                            vspace(v)
                            v.tensor_scalar(kT[:, (m - 8) * T:(m - 7) * T], kraw[:],
                                            stat[:, 2:3], None, ALU.subtract)
                            ins = v.tensor_copy(vjk[:, 0:1], vjk[:, 1:2])
                            s_ve.inc(ins)
                            M[f"kTd_{b}_{m}"] = s_ve.n
                        blk.vector(f_ve)

                    if m - 1 >= 16:
                        def f_vev(v):
                            v.wait_ge(s_ac.h, M[f"qkve_{b}_{m}"]) if False else None
                            emit_vtrans_evac(v, m - 1 - 16)
                        blk.vector(f_vev)

                def f_pe_last(t):
                    t.wait_ge(s_ac.h, M[f"qkve_{b}_{23}"])
                    emit_vtrans(t, 7)
                TB(f_pe_last)

                def f_ve_last(v):
                    emit_vtrans_evac(v, 7)
                blk.vector(f_ve_last)

                barrier()

            with nc.sbuf_tensor(f"v1{b}", [128, 2 * T], DT16) as v1, \
                 nc.sbuf_tensor(f"v2{b}", [128, T], DT16) as v2, \
                 nc.sbuf_tensor(f"r4{b}", [128, 2 * T], BF16) as r4, \
                 nc.sbuf_tensor(f"aTS{b}", [128, 2 * T], DT16) as aTS, \
                 nc.psum_tensor(f"U{b}", [128, T], F32) as U, \
                 nc.psum_tensor(f"Tp{b}", [128, 1024], F32) as Tp, \
                 nc.psum_tensor(f"Yp{b}", [128, 1024], F32) as Yp:

                for h in range(H):
                    pair, parity = h // 2, h % 2
                    qoff = pair * T
                    tp_col = (0, 64) if parity else None
                    tp_row = (64, 0) if parity else None

                    base_ac = s_ac.n
                    for qt in range(QT):
                        M[f"p1_{b}_{h}_{qt}"] = base_ac + qt + 1
                    nve = s_ve.n
                    nte = s_te.n
                    te_bases = {}
                    for i in range(QT + 3):
                        qt, qa, qv = i, i - 2, i - 3
                        if 0 <= qa <= 15:
                            te_bases[qa] = nte
                            nte += (qa + 4) // 4
                            M[f"aTSe_{b}_{h}_{qa}"] = nte
                        if 0 <= qv <= 15 and (qv % 4 == 3 or qv == 15):
                            nve += 1
                            M[f"yev_{b}_{h}_{qv//4}"] = nve
                        if qt <= 15:
                            nve += 1
                            M[f"vq_{b}_{h}_{qt}"] = nve
                    pred_ve, pred_te, pred_ac = nve, nte, base_ac + QT

                    def f_pe(t):
                        for i in range(QT + 3):
                            qt, qa, qv = i, i - 2, i - 3
                            if i >= 16:
                                t.flush()
                            if qt <= 15:
                                if qt >= 1:
                                    t.wait_ge(s_ac.h, M[f"p1_{b}_{h}_{qt-1}"])
                                elif h >= 1:
                                    t.wait_ge(s_ac.h, M[f"p1_{b}_{h-1}_15"])
                                for cc in range(4):
                                    t.matmul(U[:, cc * 512:(cc + 1) * 512],
                                             qT[parity * 64:parity * 64 + 64,
                                                qoff + qt * 128: qoff + qt * 128 + 128],
                                             kT[parity * 64:parity * 64 + 64,
                                                qoff + cc * 512: qoff + (cc + 1) * 512],
                                             start=True, stop=True, tile_position=tp_row)
                                if qt <= 1:
                                    t.mark_drain(f"U_{b}_{h}_{qt}")
                                else:
                                    t.mark(f"U_{b}_{h}_{qt}")
                            if 0 <= qa <= 15:
                                te_base = te_bases[qa]
                                t.wait_ge(s_ve.h, M[f"vq_{b}_{h}_{qa}"])
                                if te_base > 0:
                                    t.wait_ge(s_te.h, te_base)
                                for kb in range(qa + 1):
                                    j = kb // 4
                                    if j >= 2:
                                        t.wait_ge(s_te.h, te_base + j - 1)
                                    t.matmul(
                                        Tp[:, (j % 2) * 512 + (kb % 4) * 128:
                                           (j % 2) * 512 + (kb % 4) * 128 + 128],
                                        r4[:, (qa % 2) * T + kb * 128:(qa % 2) * T + kb * 128 + 128],
                                        diag[:, (qa % 2) * 128:(qa % 2) * 128 + 128],
                                        start=True, stop=True)
                                    if kb % 4 == 3 or kb == qa:
                                        t.mark(f"attT_{b}_{h}_{qa}_{kb//4}")
                            if 0 <= qv <= 15:
                                g = qv // 4
                                t.wait_ge(s_te.h, M[f"aTSe_{b}_{h}_{qv}"])
                                if qv % 4 == 0:
                                    if g >= 2:
                                        t.wait_ge(s_ve.h, M[f"yev_{b}_{h}_{g-2}"])
                                    elif h >= 1:
                                        t.wait_ge(s_ve.h, M[f"yev_{b}_{h-1}_{g+2}"])
                                for kb in range(qv + 1):
                                    first = (qv % 4 == 0 and kb == 0)
                                    last = (kb == qv and (qv % 4 == 3 or qv == 15))
                                    t.matmul(
                                        Yp[parity * 64:parity * 64 + 64,
                                           (g % 2) * 512 + (qv % 4) * 128:
                                           (g % 2) * 512 + (qv % 4) * 128 + 128],
                                        VT[:, kb * C + h * D: kb * C + h * D + D],
                                        aTS[:, (qv % 2) * T + kb * 128:(qv % 2) * T + (kb + 1) * 128],
                                        start=first, stop=last, tile_position=tp_col)
                                t.mark(f"attV_{b}_{h}_{qv}")
                    TB(f_pe)

                    def f_sc(sc):
                        for qt in range(QT):
                            sc.wait_ge(s_pe.h, M[f"U_{b}_{h}_{qt}"])
                            if qt >= 2:
                                sc.wait_ge(s_ve.h, M[f"vq_{b}_{h}_{qt-2}"])
                            elif h >= 1:
                                sc.wait_ge(s_ve.h, M[f"vq_{b}_{h-1}_{14+qt}"])
                            sc.activation(v1[:, (qt % 2) * T:(qt % 2) * T + T], U[:],
                                          AF.Relu, accum_out=mado[:, qt:qt + 1])
                            sc_mark(sc, f"p1_{b}_{h}_{qt}")
                    n0 = s_ac.n
                    s_ac.n = base_ac
                    blk.scalar(f_sc)
                    assert s_ac.n == pred_ac, (s_ac.n, pred_ac)

                    def f_ve(v):
                        for i in range(QT + 3):
                            qt, qa, qv = i, i - 2, i - 3
                            if 0 <= qa <= 15:
                                if qa >= 2:
                                    v.wait_ge(s_pe.h, M[f"attV_{b}_{h}_{qa-2}"])
                                elif h >= 1:
                                    v.wait_ge(s_pe.h, M[f"attV_{b}_{h-1}_{14+qa}"])
                                nsg = (qa + 4) // 4
                                for j in range(nsg):
                                    v.wait_ge(s_pe.h, M[f"attT_{b}_{h}_{qa}_{j}"])
                                    lo = j * 4 * 128
                                    hi = min((j + 1) * 4, qa + 1) * 128
                                    ins = v.tensor_copy(aTS[:, (qa % 2) * T + lo:(qa % 2) * T + hi],
                                                        Tp[:, (j % 2) * 512:(j % 2) * 512 + hi - lo])
                                    s_te.inc(ins)
                                assert s_te.n == M[f"aTSe_{b}_{h}_{qa}"]
                            if 0 <= qv <= 15 and (qv % 4 == 3 or qv == 15):
                                g = qv // 4
                                v.wait_ge(s_pe.h, M[f"attV_{b}_{h}_{qv}"])
                                w = (qv % 4 + 1) * 128
                                ins = v.tensor_copy(
                                    ySB[parity * 64:parity * 64 + 64,
                                        pair * T + g * 512: pair * T + g * 512 + w],
                                    Yp[parity * 64:parity * 64 + 64,
                                       (g % 2) * 512:(g % 2) * 512 + w])
                                s_ve.inc(ins)
                                assert s_ve.n == M[f"yev_{b}_{h}_{g}"]
                            if qt <= 15:
                                v.wait_ge(s_ac.h, M[f"p1_{b}_{h}_{qt}"])
                                if qt >= 2:
                                    v.wait_ge(s_pe.h, M[f"attT_{b}_{h}_{qt-2}_{(qt-2)//4}"])
                                elif h >= 1:
                                    v.wait_ge(s_pe.h, M[f"attT_{b}_{h-1}_{14+qt}_{(14+qt)//4}"])
                                vv = v1[:, (qt % 2) * T:(qt % 2) * T + T]
                                if qt > 0:
                                    v.tensor_tensor(v2[:, 0:qt * 128], vv[:, 0:qt * 128],
                                                    vv[:, 0:qt * 128], ALU.mult)
                                v.tensor_tensor(stat[:, 1:2], mado[:, qt:qt + 1], iccol[:],
                                                ALU.mult)
                                v.tensor_tensor(v2[:, qt * 128:qt * 128 + 128],
                                                vv[:, qt * 128:qt * 128 + 128], maskq[:], ALU.mult)
                                v.tensor_tensor(stat[:, 1:2], stat[:, 1:2], jecol[:], ALU.add)
                                v.tensor_tensor(v2[:, qt * 128:qt * 128 + 128],
                                                v2[:, qt * 128:qt * 128 + 128],
                                                v2[:, qt * 128:qt * 128 + 128], ALU.mult)
                                v.tensor_tensor(stat[:, 2:3], stat[:, 1:2], stat[:, 1:2], ALU.mult)
                                v.scalar_tensor_tensor(
                                    r4[:, (qt % 2) * T:(qt % 2) * T + (qt + 1) * 128],
                                    v2[:, 0:(qt + 1) * 128], 1.0, v2[:, 0:(qt + 1) * 128],
                                    ALU.mult, ALU.mult, accum_out=den16[:, qt:qt + 1])
                                v.scalar_tensor_tensor(stat[:, 3:4], stat[:, 2:3], 1e-6,
                                                       stat[:, 2:3], ALU.mult, ALU.mult)
                                vspace(v)
                                vspace(v)
                                v.tensor_tensor(stat[:, 4:5], den16[:, qt:qt + 1],
                                                stat[:, 3:4], ALU.add)
                                vspace(v)
                                v.reciprocal(inv16[:, qt:qt + 1], stat[:, 4:5])
                                vspace(v)
                                ins = v.tensor_scalar(diag[:, (qt % 2) * 128:(qt % 2) * 128 + 128],
                                                      ident[:], inv16[:, qt:qt + 1], None, ALU.mult)
                                s_ve.inc(ins)
                                assert s_ve.n == M[f"vq_{b}_{h}_{qt}"]
                    blk.vector(f_ve)
                    assert s_ve.n == pred_ve and s_te.n == pred_te

                barrier()

            with nc.sbuf_tensor(f"x0b{b}", [128, 2 * C], F32) as x0b, \
                 nc.sbuf_tensor(f"xpt{b}", [128, 2 * C], F32) as xpt, \
                 nc.sbuf_tensor(f"pscr{b}", [128, C], F32) as pscr, \
                 nc.sbuf_tensor(f"lnc2{b}", [128, 8], F32) as lnc2, \
                 nc.psum_tensor(f"pprj{b}", [128, 2 * C], F32) as pprj:

                for tt in range(NTB):
                    par = tt % 2
                    row = tok0 + tt * 128
                    gt = b * NTB + tt

                    def f_sync(sync):
                        if tt >= 2:
                            sync.wait_ge(s_ve.h, M[f"x0bu_{b}_{tt-2}"])
                        ins = sync.dma_start(out=x0b[:, par * C:(par + 1) * C],
                                             in_=x0[row:row + 128, :])
                        s_sd.inc(ins, 16)
                        M[f"x0r_{b}_{tt}"] = s_sd.n
                    blk.sync(f_sync)

                    def f_ve0(v):
                        v.wait_ge(s_sd.h, M[f"x0r_{b}_{tt}"])
                        ins = v.tensor_tensor(x0b[:, par * C:(par + 1) * C],
                                              x0b[:, par * C:(par + 1) * C], BPROJ[:], ALU.add)
                        s_ve.inc(ins)
                        M[f"x0b_{b}_{tt}"] = s_ve.n
                    blk.vector(f_ve0)

                    def f_pe(t):
                        if tt == 0:
                            t.wait_ge(s_gd.h, M["wproj_ld"])
                        if tt >= 2:
                            t.wait_ge(s_ve.h, M[f"prje_{b}_{tt-2}"])
                        for k in range(8):
                            for hf in range(2):
                                ins = t.matmul(pprj[:, par * C + hf * 512: par * C + (hf + 1) * 512],
                                               ySB[:, k * T + tt * 128: k * T + tt * 128 + 128],
                                               wsb[:, k * C + hf * 512: k * C + (hf + 1) * 512],
                                               start=(k == 0), stop=(k == 7))
                        t.mark(f"prj_{b}_{tt}")
                    TB(f_pe)

                    def f_ve(v):
                        v.wait_ge(s_pe.h, M[f"prj_{b}_{tt}"])
                        v.wait_ge(s_ve.h, M[f"x0b_{b}_{tt}"])
                        if tt >= 2:
                            v.wait_ge(s_sd.h, M[f"xps_{b}_{tt-2}"])
                            v.wait_ge(s_ac.h, M[f"ssq2_{b}_{tt-2}"])
                        ins = v.scalar_tensor_tensor(xpt[:, par * C:(par + 1) * C],
                                                     pprj[:, par * C:(par + 1) * C], 1.0,
                                                     x0b[:, par * C:(par + 1) * C],
                                                     ALU.mult, ALU.add)
                        s_ve.inc(ins)
                        M[f"xp_{b}_{tt}"] = s_ve.n
                        M[f"prje_{b}_{tt}"] = s_ve.n
                        M[f"x0bu_{b}_{tt}"] = s_ve.n
                        vspace(v)
                        mncol = mnsb[:, gt:gt + 1]
                        v.reduce_sum(mb2[:, gt:gt + 1], xpt[:, par * C:(par + 1) * C], AXX)
                        v.reduce_max(lmx[:, gt:gt + 1], xpt[:, par * C:(par + 1) * C], AXX)
                        ins = v.tensor_reduce(mncol, xpt[:, par * C:(par + 1) * C],
                                              AXX, ALU.min)
                        s_ve.inc(ins)
                        M[f"ln2r_{b}_{tt}"] = s_ve.n
                    blk.vector(f_ve)

                    def f_sync2(sync):
                        sync.wait_ge(s_ve.h, M[f"xp_{b}_{tt}"])
                        ins = sync.dma_start(out=xp[row:row + 128, :],
                                             in_=xpt[:, par * C:(par + 1) * C])
                        s_sd.inc(ins, 16)
                        M[f"xps_{b}_{tt}"] = s_sd.n
                    blk.sync(f_sync2)

                    def f_sc(sc):
                        sc.wait_ge(s_ve.h, M[f"ln2r_{b}_{tt}"])
                        sc.activation(pscr[:], xpt[:, par * C:(par + 1) * C], AF.Square,
                                      accum_out=rs2[:, gt:gt + 1])
                        sc_mark(sc, f"ssq2_{b}_{tt}")
                        if b == BL - 1 and tt == NTB - 1:
                            ins = sc.drain()
                            s_ac.inc(ins)
                            M["ssqfin"] = s_ac.n
                    blk.scalar(f_sc)

                barrier()

        with nc.psum_tensor("ptx", [32, 128], F32) as ptx, \
             nc.psum_tensor("pbx", [128, C + 512], F32) as pbx:

            mean_ = BPROJ[:, 0:NT]
            var_ = BPROJ[:, NT:2 * NT]
            mn_ = mnsb[:, 0:NT]
            t4_ = BPROJ[:, 3 * NT:4 * NT]
            sqv_ = BPROJ[:, 4 * NT:5 * NT]

            def p4_stats_ve(v):
                v.wait_ge(s_ac.h, M["ssqfin"])
                v.tensor_scalar(mean_, mb2[:, 0:NT], 1.0 / C, None, ALU.mult)
                vspace(v)
                v.tensor_tensor(t4_, mean_, mean_, ALU.mult)
                vspace(v)
                v.scalar_tensor_tensor(var_, rs2[:, 0:NT], 1.0 / C, t4_,
                                       ALU.mult, ALU.subtract)
                vspace(v)
                ins = v.tensor_scalar(var_, var_, 1e-5, None, ALU.add)
                s_ve.inc(ins)
                M["bvar"] = s_ve.n
            blk.vector(p4_stats_ve)

            def p4_stats_sc(sc):
                sc.wait_ge(s_ve.h, M["bvar"])
                ins = sc.activation(sqv_, var_, AF.Sqrt)
                s_ac.inc(ins)
                M["bsq"] = s_ac.n
            blk.scalar(p4_stats_sc)

            def p4_stats_ve2(v):
                v.wait_ge(s_ac.h, M["bsq"])
                v.reciprocal(rs2[:, 0:NT], sqv_)
                v.tensor_tensor(lmx[:], lmx[:], mean_, ALU.subtract)
                v.tensor_tensor(t4_, mean_, mn_, ALU.subtract)
                vspace(v)
                v.tensor_tensor(lmx[:], lmx[:], t4_, ALU.max)
                vspace(v)
                v.tensor_tensor(lmx[:], lmx[:], rs2[:, 0:NT], ALU.mult)
                ins = v.scalar_tensor_tensor(mb2[:, 0:NT], mean_, -1.0,
                                             rs2[:, 0:NT], ALU.mult, ALU.mult)
                s_ve.inc(ins)
                M["bstat"] = s_ve.n
            blk.vector(p4_stats_ve2)

            def p4_ve(v):
                vspace(v)
                vspace(v)
                ins = v.reduce_max(gmxc[:], lmx[:], AXX)
                s_ve.inc(ins)
                M["gmxcol"] = s_ve.n
            blk.vector(p4_ve)

            def p4_pe(t):
                t.wait_ge(s_ve.h, M["gmxcol"])
                ins = t.transpose(ptx[0:1, :], gmxc[:, 0:1], identf[:])
                t.mark_drain("gmxT")
            TB(p4_pe)

            def p4_ve2(v):
                v.wait_ge(s_pe.h, M["gmxT"])
                v.reduce_max(ccb[0:1, 0:1], ptx[0:1, 0:128], AXX)
                vspace(v)
                vspace(v)
                for j in range(1, 8):
                    ins = v.tensor_copy(ccb[0:1, j:j + 1], ccb[0:1, 0:1])
                s_ve.inc(ins)
                M["ccb"] = s_ve.n
            blk.vector(p4_ve2)

            def p4_sync(sync):
                sync.wait_ge(s_ve.h, M["ccb"])
                ins = sync.dma_start(out=cc_in[0:1, 0:8], in_=ccb[:])
                s_sd.inc(ins, 16)
                M["ccpush"] = s_sd.n
            blk.sync(p4_sync)

            def p4_gp(gp):
                gp.wait_ge(s_sd.h, M["ccpush"])
                ins = gp.collective_compute(
                    "AllReduce", ALU.max,
                    replica_groups=[[0, 1, 2, 3, 4, 5, 6, 7]],
                    ins=[cc_in[0:1, 0:8]],
                    outs=[cc_out[0:1, 0:8]],
                )
                s_cc.inc(ins)
                M["ccdone"] = s_cc.n
            blk.gpsimd(p4_gp)

            def p4_sync2(sync):
                sync.wait_ge(s_cc.h, M["ccdone"])
                ins = sync.dma_start(out=ccb[:], in_=cc_out[0:1, 0:8])
                s_sd.inc(ins, 16)
                M["ccpull"] = s_sd.n
            blk.sync(p4_sync2)

            def p4_ve3(v):
                v.wait_ge(s_sd.h, M["ccpull"])
                v.tensor_scalar(stat[0:1, 5:6], ccb[0:1, 0:1], 1.0 / 127.0, None, ALU.mult)
                vspace(v)
                v.tensor_scalar(stat[0:1, 5:6], stat[0:1, 5:6], 1e-5, None, ALU.max)
                vspace(v)
                v.reciprocal(stat[0:1, 6:7], stat[0:1, 5:6])
                vspace(v)
                ins = v.tensor_scalar(wsrw[:], wsrw[:], stat[0:1, 5:6], None, ALU.mult)
                s_ve.inc(ins)
                M["sx"] = s_ve.n
            blk.vector(p4_ve3)

            def p4_pe2(t):
                t.wait_ge(s_ve.h, M["sx"])
                for i in range(2):
                    t.matmul(pbx[:, i * 512:(i + 1) * 512], onescol[:],
                             wsrw[0:1, i * 512:(i + 1) * 512], start=True, stop=True)
                ins = t.matmul(pbx[:, C:C + 1], onescol[:], stat[0:1, 6:7],
                               start=True, stop=True)
                t.mark_drain("fscmat")
            TB(p4_pe2)

            def p4_ve4(v):
                v.wait_ge(s_pe.h, M["fscmat"])
                v.tensor_copy(FSC[:], pbx[:, 0:C])
                v.tensor_copy(isxcol[:], pbx[:, C:C + 1])
                vspace(v)
                v.tensor_scalar(lmx[:], rs2[:], isxcol[:], None, ALU.mult)
                v.tensor_scalar(mado[:], mb2[:, 0:QT], isxcol[:], None, ALU.mult)
                ins = v.tensor_scalar(den16[:], mb2[:, QT:NT], isxcol[:], None,
                                      ALU.mult)
                s_ve.inc(ins)
                M["fsc"] = s_ve.n
            blk.vector(p4_ve4)

        def p4_gpw(gp):
            ins = gp.dma_start(out=w8stg[:], in_=w8t[0])
            s_gd.inc(ins, 16)
            M["w8ld_0"] = s_gd.n
        blk.gpsimd(p4_gpw)

        for k in range(8):
            def f_w8v(v):
                v.wait_ge(s_gd.h, M[f"w8ld_{k}"])
                v.tensor_copy(wsb[:, k * C:(k + 1) * C], w8stg[:])
                ins = v.tensor_copy(vjk[:, 0:1], vjk[:, 1:2])
                s_ve.inc(ins)
                M[f"w8c_{k}"] = s_ve.n
            blk.vector(f_w8v)
            if k + 1 < 8:
                def f_w8g(gp):
                    gp.wait_ge(s_ve.h, M[f"w8c_{k}"])
                    ins = gp.dma_start(out=w8stg[:], in_=w8t[k + 1])
                    s_gd.inc(ins, 16)
                    M[f"w8ld_{k+1}"] = s_gd.n
                blk.gpsimd(f_w8g)

        barrier()

        with nc.sbuf_tensor("xpr", [128, 2 * C], F32) as xpr, \
             nc.sbuf_tensor("ln2s", [128, C], F32) as ln2s, \
             nc.sbuf_tensor("q32", [128, C], I32) as q32, \
             nc.sbuf_tensor("x8", [128, 2 * C], BF16) as x8, \
             nc.sbuf_tensor("x8T", [128, 2 * C], BF16) as x8T, \
             nc.sbuf_tensor("ffs", [128, 2 * C], F32) as ffs, \
             nc.psum_tensor("ptf", [128, 2 * C], BF16) as ptf, \
             nc.psum_tensor("pff", [128, 2 * C], F32) as pff:

            PRED = {}
            vv = s_ve.n
            for step in range(NT + 2):
                if step < NT:
                    vv += 1; PRED[f"q32_{step}"] = vv
                if 0 <= step - 1 < NT:
                    vv += 1; PRED[f"x8e_{step-1}"] = vv
                if 0 <= step - 2 < NT:
                    vv += 1; PRED[f"ffe_{step-2}"] = vv
            pp = s_pe.n
            for step in range(NT + 1):
                if step < NT:
                    pp += 1; PRED[f"x8tr_{step}"] = pp
                if 0 <= step - 1 < NT:
                    pp += 1; PRED[f"ffmm_{step-1}"] = pp
            gg = s_gp.n
            for g in range(NT):
                gg += 1; PRED[f"fbx_{g}"] = gg
            xdp = s_xd.n
            for g in range(NT):
                xdp += 16; PRED[f"xpl_{g}"] = xdp
            stp = s_st.n
            for g in range(NT):
                stp += 16; PRED[f"outs_{g}"] = stp
            M.update(PRED)

            def p5_sync(sync):
                for g in range(NT):
                    sync.wait_ge(s_ve.h, M[f"ffe_{g}"])
                    ins = sync.dma_start(out=out[g * 128:(g + 1) * 128, :],
                                         in_=ffs[:, (g % 2) * C:(g % 2) * C + C])
                    s_st.inc(ins, 16)
                sync.wait_ge(s_st.h, s_st.n)
            blk.sync(p5_sync)

            def p5_sc(sc):
                for g in range(NT):
                    if g >= 2:
                        sc.wait_ge(s_ve.h, M[f"q32_{g-2}"])
                        sc.wait_ge(s_gp.h, M[f"fbx_{g-2}"])
                    ins = sc.dma_start(out=xpr[:, (g % 2) * C:(g % 2) * C + C],
                                       in_=xp[g * 128:(g + 1) * 128, :])
                    s_xd.inc(ins, 16)
            blk.scalar(p5_sc)

            def p5_gp(gp):
                for g in range(NT):
                    gp.wait_ge(s_xd.h, M[f"xpl_{g}"])
                    if g >= 2:
                        gp.wait_ge(s_st.h, M[f"outs_{g-2}"])
                    ins = gp.tensor_tensor(ffs[:, (g % 2) * C:(g % 2) * C + C],
                                           xpr[:, (g % 2) * C:(g % 2) * C + C],
                                           FBX[:], ALU.add)
                    s_gp.inc(ins)
            blk.gpsimd(p5_gp)

            def p5_ve(v):
                for step in range(NT + 2):
                    g0, g1, g2 = step, step - 1, step - 2
                    if g0 < NT:
                        p0 = (g0 % 2) * C
                        v.wait_ge(s_xd.h, M[f"xpl_{g0}"])
                        if g0 >= 2:
                            v.wait_ge(s_pe.h, M[f"x8tr_{g0-2}"])
                        mbx = mado[:, g0:g0 + 1] if g0 < QT \
                            else den16[:, g0 - QT:g0 - QT + 1]
                        v.tensor_scalar(ln2s[:], xpr[:, p0:p0 + C],
                                        lmx[:, g0:g0 + 1], None, ALU.mult)
                        vspace(v)
                        v.tensor_scalar(q32[:], ln2s[:], mbx, 0.0,
                                        ALU.add, ALU.add)
                        vspace(v)
                        ins = v.tensor_copy(x8[:, p0:p0 + C], q32[:])
                        s_ve.inc(ins)
                    if 0 <= g1 < NT:
                        p1 = (g1 % 2) * C
                        v.wait_ge(s_pe.h, M[f"x8tr_{g1}"])
                        if g1 >= 2:
                            v.wait_ge(s_pe.h, M[f"ffmm_{g1-2}"])
                        ins = v.tensor_copy(x8T[:, p1:p1 + C], ptf[:, p1:p1 + C])
                        s_ve.inc(ins)
                    if 0 <= g2 < NT:
                        p2 = (g2 % 2) * C
                        v.wait_ge(s_pe.h, M[f"ffmm_{g2}"])
                        v.wait_ge(s_gp.h, M[f"fbx_{g2}"])
                        v.tensor_tensor(BPROJ[:], pff[:, p2:p2 + C], FSC[:],
                                        ALU.mult)
                        vspace(v)
                        ins = v.tensor_tensor(ffs[:, p2:p2 + C],
                                              ffs[:, p2:p2 + C], BPROJ[:],
                                              ALU.add)
                        s_ve.inc(ins)
            blk.vector(p5_ve)

            def p5_pe(t):
                for step in range(NT + 1):
                    g0, g1 = step, step - 1
                    if g0 < NT:
                        p0 = (g0 % 2) * C
                        t.wait_ge(s_ve.h, M[f"q32_{g0}"])
                        if g0 >= 2:
                            t.wait_ge(s_ve.h, M[f"x8e_{g0-2}"])
                        for c in range(8):
                            t.transpose(ptf[:, p0 + c * 128:p0 + (c + 1) * 128],
                                        x8[:, p0 + c * 128:p0 + (c + 1) * 128],
                                        ident[:])
                        t.mark(f"x8tr_{g0}")
                    if 0 <= g1 < NT:
                        p1 = (g1 % 2) * C
                        t.wait_ge(s_ve.h, M[f"x8e_{g1}"])
                        if g1 >= 2:
                            t.wait_ge(s_ve.h, M[f"ffe_{g1-2}"])
                        for k in range(8):
                            for hf in range(2):
                                t.matmul(pff[:, p1 + hf * 512:p1 + (hf + 1) * 512],
                                         x8T[:, p1 + k * 128:p1 + (k + 1) * 128],
                                         wsb[:, k * C + hf * 512:k * C + (hf + 1) * 512],
                                         start=(k == 0), stop=(k == 7))
                        if g1 == NT - 1:
                            t.mark_drain(f"ffmm_{g1}")
                        else:
                            t.mark(f"ffmm_{g1}")
            TB(p5_pe)

            for key, val in PRED.items():
                assert M[key] == val, (key, M[key], val)
            assert s_ve.n == vv and s_pe.n == pp and s_gp.n == gg
            assert s_xd.n == xdp and s_st.n == stp

    return nc


_NC_CACHE = {}


def kernel(L0_attn_x, L0_attn_W_attn, L0_attn_b_attn, L0_attn_W_proj, L0_attn_b_proj,
           L0_attn_score_gain, L0_ffn_weight_int8, L0_ffn_weight_scale, L0_ffn_bias):
    x = np.ascontiguousarray(np.asarray(L0_attn_x, dtype=np.float32))
    W = np.asarray(L0_attn_W_attn, dtype=np.float32)
    ba = np.ascontiguousarray(np.asarray(L0_attn_b_attn, dtype=np.float32).reshape(3 * C, 1))
    Wp = np.asarray(L0_attn_W_proj, dtype=np.float32)
    bp = np.ascontiguousarray(np.asarray(L0_attn_b_proj, dtype=np.float32).reshape(1, C))
    g = np.ascontiguousarray(np.asarray(L0_attn_score_gain, dtype=np.float32).reshape(1, 1))
    w8 = np.asarray(L0_ffn_weight_int8)
    ws = np.ascontiguousarray(np.asarray(L0_ffn_weight_scale, dtype=np.float32).reshape(1, C))
    fb = np.ascontiguousarray(np.asarray(L0_ffn_bias, dtype=np.float32).reshape(1, C))

    wattn_m = np.ascontiguousarray(
        W.reshape(8, 128, 24, 128).transpose(2, 1, 0, 3).reshape(24, 128, C))
    wproj_m = np.ascontiguousarray(Wp.reshape(8, 128, C))
    w8t = np.ascontiguousarray(w8.T.reshape(8, 128, C))

    if "nc" not in _NC_CACHE:
        _NC_CACHE["nc"] = build()
    nc = _NC_CACHE["nc"]

    in_maps = []
    for c in range(8):
        in_maps.append({
            "x0": np.ascontiguousarray(x[2 * c:2 * c + 2].reshape(NTOK, C)),
            "wattn_m": wattn_m, "battn": ba, "wproj_m": wproj_m, "bprow": bp,
            "grow": g, "w8t": w8t, "wsrow": ws, "fbrow": fb,
        })
    res = run_bass_kernel_spmd(nc, in_maps, core_ids=list(range(8)))
    outs = [np.asarray(res.results[c]["out"]).reshape(2, T, C) for c in range(8)]
    return np.concatenate(outs, axis=0).astype(np.float32)
